# version 20
# baseline (speedup 1.0000x reference)
import numpy as np
import ml_dtypes
from contextlib import ExitStack

import concourse.bass as bass
import concourse.mybir as mybir
from concourse.bass_utils import run_bass_kernel_spmd

F32 = mybir.dt.float32
BF16 = mybir.dt.bfloat16
U8 = mybir.dt.uint8
U32 = mybir.dt.uint32
AF = mybir.ActivationFunctionType
ALU = mybir.AluOpType
AX = mybir.AxisListType

NCORE = 8
S = 8192
D = 2048
T = S // NCORE
NT = T // 128
E = 8
FI = 1024
SCALE = 1.0 / float(np.sqrt(128.0))
EPS = 1e-5
KB = 1024

ENGS = ("pe", "act", "dve", "pool", "sp")
NDMASEM = 8
SEMS = (("pe", "act", "dve", "pool", "cc")
        + tuple(f"poolq{i}" for i in range(NDMASEM))
        + tuple(f"sp{i}" for i in range(NDMASEM)))

DBG = set()
PHASE_LIMIT = 99


class Prog:
    def __init__(self):
        self.steps = {e: [] for e in ENGS}
        self.cnt = {s: 0 for s in SEMS}
        self.awaited = {s: set() for s in SEMS}
        self.dma_n = {"poolq": 0, "sp": 0}

    def add(self, eng, emit, deps=(), sem=None, inc=1, always_inc=False):
        sem = sem or eng
        dl = []
        for d in deps:
            if d is None:
                continue
            ps, c = d
            if c <= 0:
                continue
            dl.append(d)
            self.awaited[ps].add(c)
        self.cnt[sem] += inc
        self.steps[eng].append((dl, emit, sem, self.cnt[sem], always_inc, inc))
        return (sem, self.cnt[sem])

    def dma(self, eng, emit, deps=()):
        q = "poolq" if eng == "pool" else "sp"
        i = self.dma_n[q]
        self.dma_n[q] = i + 1
        sem = f"{q}{i % NDMASEM}"
        prior = (sem, 16 * (i // NDMASEM))
        return self.add(eng, emit, list(deps) + [prior], sem=sem, inc=16, always_inc=True)

    def dma_all(self, q):
        return [(f"{q}{j}", self.cnt[f"{q}{j}"]) for j in range(NDMASEM)]

    def cc(self, emit, deps=()):
        return self.add("pool", emit, deps, sem="cc", inc=1, always_inc=True)

    def now(self):
        return [(s, self.cnt[s]) for s in SEMS if self.cnt[s] > 0]

    def emit_engine(self, eng, obj, semobjs):
        waited = {}
        emitted = {s: 0 for s in SEMS}
        for (dl, emit, sem, cnt_after, always, inc) in self.steps[eng]:
            for (ps, c) in dl:
                if waited.get(ps, 0) >= c:
                    continue
                obj.wait_ge(semobjs[ps], c)
                waited[ps] = c
            inst = emit(obj)
            if always or (cnt_after in self.awaited[sem]):
                delta = cnt_after - emitted[sem]
                assert delta > 0
                if always:
                    assert delta == inc
                if inst is None:
                    obj.sem_inc(semobjs[sem], delta)
                else:
                    inst.then_inc(semobjs[sem], delta)
                emitted[sem] = cnt_after


def _dtsz(dt):
    return 2 if dt == BF16 else 4


class Arena:
    def __init__(self, ap, nbytes):
        self.ap = ap
        self.nbytes = nbytes

    def carve(self, off_kb, shape, dtype):
        off = int(round(off_kb * KB))
        part = int(shape[0])
        fsize = int(np.prod(shape[1:])) * _dtsz(dtype) if len(shape) > 1 else _dtsz(dtype)
        assert off % 32 == 0, off_kb
        assert off + fsize <= self.nbytes, (off_kb, shape, dtype)
        v = self.ap[0:part, off:off + fsize].bitcast(dtype)
        if len(shape) <= 2:
            return v
        nf = len(shape) - 1
        pat = "p (" + " ".join(f"d{i}" for i in range(nf)) + ") -> p " + \
            " ".join(f"d{i}" for i in range(nf))
        kw = {f"d{i}": int(shape[1 + i]) for i in range(nf)}
        return v.rearrange(pat, **kw)


def build_nc():
    nc = bass.Bass("TRN2", target_bir_lowering=False, debug=False)

    x_my = nc.dram_tensor("x_my", [T, D], F32, kind="ExternalInput").ap()
    wq_t = nc.dram_tensor("wq_t", [D, 256], F32, kind="ExternalInput").ap()
    wk_t = nc.dram_tensor("wk_t", [D, 256], F32, kind="ExternalInput").ap()
    wv_t = nc.dram_tensor("wv_t", [D, 256], F32, kind="ExternalInput").ap()
    bq_d = nc.dram_tensor("bq", [1, 256], F32, kind="ExternalInput").ap()
    bk_d = nc.dram_tensor("bk", [1, 256], F32, kind="ExternalInput").ap()
    bv_d = nc.dram_tensor("bv", [1, 256], F32, kind="ExternalInput").ap()
    slopes2 = nc.dram_tensor("slopes2", [1, 2], F32, kind="ExternalInput").ap()
    w_router = nc.dram_tensor("w_router", [D, 8], F32, kind="ExternalInput").ap()
    w_gu = nc.dram_tensor("w_gu", [E, D, 2 * FI], F32, kind="ExternalInput").ap()
    w_dn = nc.dram_tensor("w_dn", [E, FI, D], F32, kind="ExternalInput").ap()
    w_sg = nc.dram_tensor("w_sg", [D, FI], F32, kind="ExternalInput").ap()
    w_su = nc.dram_tensor("w_su", [D, FI], F32, kind="ExternalInput").ap()
    w_sd = nc.dram_tensor("w_sd", [FI, D], F32, kind="ExternalInput").ap()
    wout_t = nc.dram_tensor("wout_t", [D, D], F32, kind="ExternalInput").ap()
    bout_d = nc.dram_tensor("bout", [1, D], F32, kind="ExternalInput").ap()
    gidx = nc.dram_tensor("gidx", [1, 128], U32, kind="ExternalInput").ap()
    y = nc.dram_tensor("y", [T, D], F32, kind="ExternalOutput").ap()

    ag_xn_in = nc.dram_tensor("ag_xn_in", [D, T], BF16).ap()
    ag_xn_out = nc.dram_tensor("ag_xn_out", [NCORE * D, T], BF16, addr_space="Shared").ap()
    ag_at_in = nc.dram_tensor("ag_at_in", [S, 256], BF16).ap()
    ag_at_out = nc.dram_tensor("ag_at_out", [NCORE * S, 256], BF16, addr_space="Shared").ap()
    at_compact = nc.dram_tensor("at_compact", [S, 256], BF16).ap()

    dbg = {}
    if "xn" in DBG:
        dbg["xn"] = nc.dram_tensor("dbg_xn", [D, T], BF16, kind="ExternalOutput").ap()
    if "lg" in DBG:
        dbg["lg"] = nc.dram_tensor("dbg_lg", [8, T], F32, kind="ExternalOutput").ap()
    if "attnin" in DBG:
        dbg["attnin"] = nc.dram_tensor("dbg_attnin", [S, 256], BF16, kind="ExternalOutput").ap()
    if "fl" in DBG:
        dbg["btab"] = nc.dram_tensor("dbg_btab", [128, 2 * 16 * 64], F32, kind="ExternalOutput").ap()
        dbg["a2"] = nc.dram_tensor("dbg_a2", [128, 1024], F32, kind="ExternalOutput").ap()
        dbg["lrow"] = nc.dram_tensor("dbg_lrow", [1, 512], F32, kind="ExternalOutput").ap()
        dbg["rcol"] = nc.dram_tensor("dbg_rcol", [128, 4], F32, kind="ExternalOutput").ap()
        dbg["atsr"] = nc.dram_tensor("dbg_atsr", [128, 512], BF16, kind="ExternalOutput").ap()
        dbg["atsb"] = nc.dram_tensor("dbg_atsb", [128, 512], BF16, kind="ExternalOutput").ap()
        dbg["phat"] = nc.dram_tensor("dbg_phat", [128, 1024], BF16, kind="ExternalOutput").ap()
        dbg["ssb"] = nc.dram_tensor("dbg_ssb", [128, 4096], F32, kind="ExternalOutput").ap()
        dbg["nrm"] = nc.dram_tensor("dbg_nrm", [1, 8], F32, kind="ExternalOutput").ap()
    if "qkv" in DBG:
        dbg["q"] = nc.dram_tensor("dbg_q", [128, 2 * S], BF16, kind="ExternalOutput").ap()
        dbg["k"] = nc.dram_tensor("dbg_k", [128, 2 * S], BF16, kind="ExternalOutput").ap()
        dbg["v"] = nc.dram_tensor("dbg_v", [128, 64 * 256], BF16, kind="ExternalOutput").ap()
    if "probs" in DBG:
        dbg["probs"] = nc.dram_tensor("dbg_probs", [128, NT * 8], F32, kind="ExternalOutput").ap()
    if "comb" in DBG:
        dbg["comb"] = nc.dram_tensor("dbg_comb", [128, NT * D], F32, kind="ExternalOutput").ap()

    ARENA_KB = 176

    with ExitStack() as ctx:
        ec = ctx.enter_context
        arena_t = ec(nc.sbuf_tensor("arena", [128, ARENA_KB * KB], U8))
        A = Arena(arena_t.ap(), ARENA_KB * KB)

        psS = [ec(nc.psum_tensor("psS0", [128, 1024], F32)).ap(),
               ec(nc.psum_tensor("psS1", [128, 1024], F32)).ap()]
        psAV = [ec(nc.psum_tensor("psAV0", [128, 512], F32)).ap(),
                ec(nc.psum_tensor("psAV1", [128, 512], F32)).ap()]
        psL = ec(nc.psum_tensor("psL", [128, 512], F32)).ap()
        psG = [psS[0][:, 0:512], psS[1][:, 0:512],
               psS[0][:, 512:1024], psS[1][:, 512:1024]]
        psT = ec(nc.psum_tensor("psT", [128, 1024], BF16)).ap()

        sems = {s: ec(nc.semaphore("sem_" + s)) for s in SEMS}
        block = ec(nc.Block())
        P = Prog()

        xs_t = [A.carve(0, [128, D], F32), A.carve(8, [128, D], F32)]
        xnbf2 = [A.carve(16, [128, D], BF16), A.carve(20, [128, D], BF16)]
        lob2 = [A.carve(24, [128, D], BF16), A.carve(28, [128, D], BF16)]
        mvcol = A.carve(173.5625, [128, 12], F32)
        lo_scr = [A.carve(155, [128, 128], BF16), A.carve(155.25, [128, 128], BF16),
                  A.carve(155.5, [128, 128], BF16), A.carve(155.75, [128, 128], BF16)]
        xnfm = A.carve(32, [128, 16, T], BF16)
        q_loc = A.carve(0, [128, 2, S], BF16)
        k_loc = A.carve(32, [128, 2, S], BF16)
        v_loc = A.carve(64, [128, 64, 256], BF16)
        xng = [A.carve(96, [128, 16, 512], BF16), A.carve(112, [128, 16, 512], BF16)]
        wqkv_sb = {"q": A.carve(128, [128, 16, 256], BF16),
                   "k": A.carve(136, [128, 16, 256], BF16),
                   "v": A.carve(144, [128, 16, 256], BF16)}
        sqn_t = A.carve(96, [128, 1024], F32)
        sqn_b = A.carve(100, [128, 1024], BF16)
        btab = A.carve(128, [128, 2, 16, 64], F32)
        a2 = [A.carve(136, [128, 1024], F32), A.carve(140, [128, 1024], F32)]
        phat = [A.carve(144, [128, 1024], BF16), A.carve(146, [128, 1024], BF16),
                A.carve(148, [128, 1024], BF16), A.carve(150, [128, 1024], BF16)]
        ssb = [A.carve(120, [128, 1024], F32), A.carve(124, [128, 1024], F32)]
        ats_raw = A.carve(152, [128, 512], BF16)
        at_sb = A.carve(153, [128, 4, 128], BF16)
        lrow = A.carve(154, [1, 512], F32)
        rcol_sb = A.carve(154.0625, [128, 4], F32)
        xnfm2 = A.carve(0, [128, 16, T], BF16)
        comb = A.carve(32, [128, NT, D], F32)
        wgu_sb = [A.carve(96, [128, 16, 256], BF16), A.carve(104, [128, 16, 256], BF16)]
        wdn_sb = [A.carve(112, [128, 8, 512], BF16), A.carve(120, [128, 8, 512], BF16)]
        lgt_tm = A.carve(175.75, [128, 8], F32)
        m8_t = A.carve(175.78125, [128, 8], F32)
        sig_t = A.carve(175.8125, [128, 8], F32)
        tmp_f = A.carve(144, [128, 512], F32)
        p_bc = A.carve(128, [128, T], F32)
        p_rows = A.carve(167, [1, T], F32)
        h_e = A.carve(132, [128, 8, T], BF16)
        a_stage = A.carve(148, [128, 4, T], BF16)
        g_sb = A.carve(0, [128, NT, D], BF16)
        gbuf = A.carve(32, [128, 16384], BF16)
        atfm = A.carve(48, [128, 16, T], BF16)
        wout_sb = [A.carve(80, [128, 16, 512], BF16), A.carve(96, [128, 16, 512], BF16)]
        xsl_t = [A.carve(112, [128, 512], F32), A.carve(114, [128, 512], F32)]
        out_t = [A.carve(116, [128, 512], F32), A.carve(118, [128, 512], F32)]
        bout_sb = A.carve(148, [1, D], BF16)
        id_bf = A.carve(156, [128, 128], BF16)
        id_f32 = A.carve(156.5, [128, 128], F32)
        iota_kq = A.carve(157, [128, 512], F32)
        a_alibi = [A.carve(159, [128, 512], F32), A.carve(161, [128, 512], F32)]
        m01 = [A.carve(163 + 0.0, [128, 512], BF16), A.carve(164, [128, 512], BF16),
               A.carve(165, [128, 512], BF16), A.carve(166, [128, 512], BF16)]
        lg_sb = A.carve(167, [8, T], F32)
        probs_tm = A.carve(171, [128, NT, 8], F32)
        ones_bf = A.carve(171.25, [1, 512], BF16)
        ones_f32 = A.carve(172.25, [1, 128], F32)
        ones_col = A.carve(172.75, [128, 1], BF16)
        slopes_sb = A.carve(173, [1, 2], F32)
        slopes_bc = A.carve(173.03125, [128, 2], F32)
        iota_r64 = A.carve(173.0625, [1, 64], F32)
        iota_r16 = A.carve(173.3125, [1, 16], F32)
        c2row = A.carve(173.40625, [1, 2, 16], F32)
        nrm = A.carve(173.53125, [1, 8], F32)
        mcol = A.carve(173.5625, [128, 1], F32)
        vcol = A.carve(173.59375, [128, 1], F32)
        rstd = A.carve(173.625, [128, 1], F32)
        idxs_sb = A.carve(173.65625, [128, 1], U32)
        bq_sb = A.carve(173.75, [1, 256], BF16)
        bk_sb = A.carve(174.25, [1, 256], BF16)
        bv_sb = A.carve(174.75, [1, 256], BF16)
        wr_hi = A.carve(175.25, [128, 16, 8], BF16)
        wr_lo = A.carve(175.5, [128, 16, 8], BF16)
        wr_f32 = A.carve(152, [128, 16, 8], F32)
        btr_row = A.carve(150, [1, 64], F32)

        def MM(out, lhsT, rhs, start, stop):
            return lambda e: e.matmul(out=out, lhsT=lhsT, rhs=rhs, start=start, stop=stop)

        def TR(out, in_, ident):
            return lambda e: e.transpose(out=out, in_=in_, identity=ident)

        def CP(out, in_):
            return lambda e: e.tensor_copy(out=out, in_=in_)

        def ACP(out, in_):
            return lambda e: e.activation(out=out, in_=in_, func=AF.Copy)

        def ACTF(out, in_, func, **kw):
            return lambda e: e.activation(out=out, in_=in_, func=func, **kw)

        def TT(out, a, b, op):
            return lambda e: e.tensor_tensor(out=out, in0=a, in1=b, op=op)

        def TS(out, in0, s1, op0, s2=None, op1=None):
            if op1 is None:
                return lambda e: e.tensor_scalar(out=out, in0=in0, scalar1=s1, scalar2=None, op0=op0)
            return lambda e: e.tensor_scalar(out=out, in0=in0, scalar1=s1, scalar2=s2,
                                             op0=op0, op1=op1)

        def STT(out, in0, s, in1, op0, op1):
            return lambda e: e.scalar_tensor_tensor(out=out, in0=in0, scalar=s, in1=in1,
                                                    op0=op0, op1=op1)

        def RSUM(out, in_):
            return lambda e: e.reduce_sum(out=out, in_=in_, axis=AX.X)

        def RMAX(out, in_):
            return lambda e: e.reduce_max(out=out, in_=in_, axis=AX.X)

        def RECIP(out, in_):
            return lambda e: e.reciprocal(out=out, in_=in_)

        def DMA(out, in_):
            return lambda e: e.dma_start(out=out, in_=in_)

        ev_io1 = P.add("pool", lambda e: e.iota(out=id_f32, pattern=[[1, 128]], base=0,
                                                channel_multiplier=-1,
                                                allow_small_or_imprecise_dtypes=True))
        ev_idf = P.add("dve", TS(id_f32, id_f32, 0.0, ALU.is_equal), deps=[ev_io1])
        ev_idb = P.add("dve", CP(id_bf, id_f32), deps=[ev_idf])
        ev_io2 = P.add("pool", lambda e: e.iota(out=iota_kq, pattern=[[-1, 512]], base=0,
                                                channel_multiplier=1,
                                                allow_small_or_imprecise_dtypes=True))

        ev_io3 = P.add("pool", lambda e: e.iota(out=iota_r64, pattern=[[128, 64]], base=0,
                                                channel_multiplier=0,
                                                allow_small_or_imprecise_dtypes=True))
        ev_io4 = P.add("pool", lambda e: e.iota(out=iota_r16, pattern=[[512, 16]], base=0,
                                                channel_multiplier=0,
                                                allow_small_or_imprecise_dtypes=True))
        ev_on1 = P.add("pool", lambda e: e.memset(ones_bf, 1.0))
        ev_on2 = P.add("pool", lambda e: e.memset(ones_f32, 1.0))
        ev_on3 = P.add("pool", lambda e: e.memset(ones_col, 1.0))
        ev_m01 = [P.add("dve", TS(m01[j], iota_kq, float(-128 * j), ALU.is_gt,
                                  -1e9, ALU.mult), deps=[ev_io2])
                  for j in range(4)]
        ev_slo = P.dma("pool", DMA(slopes_sb, slopes2))
        ev_sb0 = P.add("pe", MM(psG[0][:, 0:2], ones_f32, slopes_sb, True, True),
                       deps=[ev_on2, ev_slo])
        ev_sbc = P.add("dve", CP(slopes_bc, psG[0][:, 0:2]), deps=[ev_sb0])
        ev_alibi = [None, None]
        psg_free = [ev_sbc, None, None, None]

        ev_bq = P.dma("pool", DMA(bq_sb, bq_d))
        ev_bk = P.dma("pool", DMA(bk_sb, bk_d))
        ev_bv = P.dma("pool", DMA(bv_sb, bv_d))
        ev_wrf = P.dma("pool", DMA(wr_f32, w_router.rearrange("(a p) c -> p a c", p=128)))
        ev_wrh = P.dma("pool", DMA(wr_hi, w_router.rearrange("(a p) c -> p a c", p=128)))
        ev_wrl = P.add("dve", STT(wr_lo, wr_f32, 1.0, wr_hi, ALU.mult, ALU.subtract),
                       deps=[ev_wrf, ev_wrh])

        ev_xld = [None] * 4
        ev_x_use = [None] * 4
        ev_xnbf_use = [None] * 4
        ev_lob_use = [None] * 4
        psT_free = [None] * 8
        ev_lg = [None] * NT
        ev_xn_w = None
        ev_loscr_use = [None, None, None, None]
        xs4 = [A.carve(0, [128, D], F32), A.carve(8, [128, D], F32),
               A.carve(16, [128, D], F32), A.carve(24, [128, D], F32)]
        xnbf4 = [A.carve(64, [128, D], BF16), A.carve(68, [128, D], BF16),
                 A.carve(72, [128, D], BF16), A.carve(76, [128, D], BF16)]
        lob4 = [A.carve(80, [128, D], BF16), A.carve(84, [128, D], BF16),
                A.carve(88, [128, D], BF16), A.carve(92, [128, D], BF16)]
        ev_sq4 = [None] * 4
        ev_lo4 = [None] * 4
        ev_xnb4 = [None] * 4

        def ln_a(tt):
            sl = tt % 4
            mcol2 = mvcol[:, sl:sl + 1]
            vcol2 = mvcol[:, 4 + sl:5 + sl]
            ev_xld[sl] = P.dma("sp", DMA(xs4[sl], x_my[tt * 128:(tt + 1) * 128, :]),
                               deps=[ev_x_use[sl]])
            ev_rs = P.add("dve", RSUM(mcol2, xs4[sl]), deps=[ev_xld[sl]])
            ev_m = P.add("dve", TS(mcol2, mcol2, 1.0 / D, ALU.mult), deps=[ev_rs])
            ev_xc = P.add("dve", TS(xs4[sl], xs4[sl], mcol2, ALU.subtract), deps=[ev_m])
            ev_sq4[sl] = P.add("act", ACTF(xnbf4[sl], xs4[sl], AF.Square, accum_out=vcol2),
                               deps=[ev_xc, ev_xnbf_use[sl]])

        def ln_b(tt):
            sl = tt % 4
            vcol2 = mvcol[:, 4 + sl:5 + sl]
            rcol2 = mvcol[:, 8 + sl:9 + sl]
            ev_var = P.add("dve", TS(vcol2, vcol2, 1.0 / D, ALU.mult, EPS, ALU.add),
                           deps=[ev_sq4[sl]])
            ev_sd = P.add("act", ACTF(vcol2, vcol2, AF.Sqrt), deps=[ev_var])
            ev_rst = P.add("dve", RECIP(rcol2, vcol2), deps=[ev_sd])
            ev_xnb4[sl] = P.add("dve", TS(xnbf4[sl], xs4[sl], rcol2, ALU.mult),
                                deps=[ev_rst, ev_sq4[sl]])
            ev_lo4[sl] = P.add("dve", STT(lob4[sl], xs4[sl], rcol2, xnbf4[sl],
                                          ALU.mult, ALU.subtract),
                               deps=[ev_xnb4[sl], ev_lob_use[sl]])
            ev_x_use[sl] = ev_lo4[sl]

        def tr_c(tt):
            nonlocal ev_xn_w
            sl = tt % 4
            last_mm = None
            last_trh = None
            last_trl = None
            for dc in range(16):
                sh = dc % 4
                slo = 4 + dc % 4
                pT_hi = psT[:, sh * 128:(sh + 1) * 128]
                pT_lo = psT[:, slo * 128:(slo + 1) * 128]
                ev_t1 = P.add("pe", TR(pT_hi, xnbf4[sl][:, dc * 128:(dc + 1) * 128], id_bf),
                              deps=[ev_xnb4[sl], ev_idb, psT_free[sh]])
                last_trh = ev_t1
                ev_e1 = P.add("dve", CP(xnfm[:, dc, tt * 128:(tt + 1) * 128], pT_hi),
                              deps=[ev_t1])
                psT_free[sh] = ev_e1
                ev_xn_w = ev_e1
                ev_t2 = P.add("pe", TR(pT_lo, lob4[sl][:, dc * 128:(dc + 1) * 128], id_bf),
                              deps=[ev_lo4[sl], ev_idb, psT_free[slo]])
                last_trl = ev_t2
                losl = dc % 4
                ev_e2 = P.add("act", ACP(lo_scr[losl], pT_lo),
                              deps=[ev_t2, ev_loscr_use[losl]])
                psT_free[slo] = ev_e2
                xnfm_chunk = xnfm[:, dc, tt * 128:(tt + 1) * 128]
                gbank = psG[tt % 4]
                m1 = P.add("pe", MM(gbank[0:8, 0:128], wr_hi[:, dc, :], xnfm_chunk,
                                    dc == 0, False),
                           deps=[ev_e1, ev_wrh, psg_free[tt % 4] if dc == 0 else None])
                m2 = P.add("pe", MM(gbank[0:8, 0:128], wr_lo[:, dc, :], xnfm_chunk,
                                    False, False), deps=[ev_wrl])
                m3 = P.add("pe", MM(gbank[0:8, 0:128], wr_hi[:, dc, :], lo_scr[losl],
                                    False, dc == 15), deps=[ev_e2])
                ev_loscr_use[losl] = m3
                last_mm = m3
            ev_lgv = P.add("dve", CP(lg_sb[:, tt * 128:(tt + 1) * 128],
                                     psG[tt % 4][0:8, 0:128]), deps=[last_mm])
            ev_lg[tt] = ev_lgv
            psg_free[tt % 4] = ev_lgv
            ev_xnbf_use[sl] = last_trh
            ev_lob_use[sl] = last_trl

        for tt in range(NT):
            ln_a(tt)
            if tt >= 1:
                ln_b(tt - 1)
            if tt >= 2:
                tr_c(tt - 2)
        ln_b(NT - 1)
        tr_c(NT - 2)
        tr_c(NT - 1)

        ev_xn_dma = P.dma("sp", DMA(ag_xn_in.rearrange("(a p) t -> p a t", p=128), xnfm),
                          deps=[ev_xn_w])
        ev_cc_xn = P.cc(lambda e: e.collective_compute(
            "AllGather", ALU.bypass, replica_groups=[list(range(NCORE))],
            ins=[ag_xn_in.opt()], outs=[ag_xn_out.opt()]), deps=[ev_xn_dma])
        if "xn" in DBG:
            P.dma("sp", DMA(dbg["xn"].rearrange("(a p) t -> p a t", p=128), xnfm),
                  deps=[ev_xn_w])
        finish_evs = [ev_cc_xn]
        if "lg" in DBG:
            P.dma("sp", DMA(dbg["lg"], lg_sb), deps=[ev_lg[NT - 1]])

        if PHASE_LIMIT <= 1:
            P.add("pool", lambda e: None, deps=finish_evs + [ev_lg[NT - 1]])
            _emit(P, block, sems)
            return nc

        ev_wq = P.dma("pool", DMA(wqkv_sb["q"], wq_t.rearrange("(a p) c -> p a c", p=128)))
        ev_wk = P.dma("pool", DMA(wqkv_sb["k"], wk_t.rearrange("(a p) c -> p a c", p=128)))
        ev_wv = P.dma("pool", DMA(wqkv_sb["v"], wv_t.rearrange("(a p) c -> p a c", p=128)))

        ev_xng = [None, None]
        ev_xng_use = [None, None]
        evQ = [[None] * 16, [None] * 16]
        evK = [[None] * 16, [None] * 16]
        evV = [None] * 64
        qk_plan = [("q", 0, bq_sb), ("q", 1, bq_sb), ("k", 0, bk_sb), ("k", 1, bk_sb)]
        for g in range(16):
            sl = g % 2
            r = g // 2
            cb = (g % 2) * 512
            r = g // 2
            r = (g * 512) // T
            cb = (g * 512) % T
            ev_xng[sl] = P.dma("sp", DMA(xng[sl],
                                         ag_xn_out[r * D:(r + 1) * D, cb:cb + 512]
                                         .rearrange("(a p) t -> p a t", p=128)),
                               deps=[ev_xng_use[sl], ev_cc_xn])
            last_use = None
            for i, (nm, s, brow) in enumerate(qk_plan):
                bank = psG[i % 4]
                last = None
                for dc in range(16):
                    dep0 = [ev_xng[sl], ev_wq, ev_wk, psg_free[i % 4]] if dc == 0 else []
                    last = P.add("pe", MM(bank[:, :], wqkv_sb[nm][:, dc, s * 128:(s + 1) * 128],
                                          xng[sl][:, dc, :], dc == 0, False), deps=dep0)
                last = P.add("pe", MM(bank[:, :], brow[0:1, s * 128:(s + 1) * 128], ones_bf,
                                      False, True), deps=[ev_bq, ev_bk, ev_on1])
                dst = (q_loc if nm == "q" else k_loc)[:, s, g * 512:(g + 1) * 512]
                if i % 2 == 0:
                    ev_ev = P.add("dve", CP(dst, bank[:, :]), deps=[last])
                else:
                    ev_ev = P.add("act", ACP(dst, bank[:, :]), deps=[last])
                psg_free[i % 4] = ev_ev
                if nm == "q":
                    evQ[s][g] = ev_ev
                else:
                    evK[s][g] = ev_ev
                last_use = ev_ev
            for vt in range(4):
                bank = psG[vt % 4]
                last = None
                for dc in range(16):
                    dep0 = [ev_xng[sl], ev_wv, psg_free[vt % 4]] if dc == 0 else []
                    last = P.add("pe", MM(bank[:, 0:256], xng[sl][:, dc, vt * 128:(vt + 1) * 128],
                                          wqkv_sb["v"][:, dc, :], dc == 0, False), deps=dep0)
                last = P.add("pe", MM(bank[:, 0:256], ones_bf[0:1, 0:128], bv_sb, False, True),
                             deps=[ev_bv])
                dst = v_loc[:, g * 4 + vt, :]
                if vt % 2 == 0:
                    ev_ev = P.add("dve", CP(dst, bank[:, 0:256]), deps=[last])
                else:
                    ev_ev = P.add("act", ACP(dst, bank[:, 0:256]), deps=[last])
                psg_free[vt % 4] = ev_ev
                evV[g * 4 + vt] = ev_ev
                last_use = ev_ev
            ev_xng_use[sl] = last_use

        ev_qdump_evs = []
        if "qkv" in DBG:
            ev_qdump_evs.append(P.dma("sp", DMA(dbg["q"], q_loc.rearrange("p a t -> p (a t)")),
                                      deps=[evQ[1][15]]))
            ev_qdump_evs.append(P.dma("sp", DMA(dbg["k"], k_loc.rearrange("p a t -> p (a t)")),
                                      deps=[evK[1][15]]))
            ev_qdump_evs.append(P.dma("sp", DMA(dbg["v"], v_loc.rearrange("p a t -> p (a t)")),
                                      deps=[evV[63]]))

        ev_sqn_deps = [ev_xng_use[0], ev_xng_use[1]]
        ev_sqn_use = None
        ev_psl_use = None
        ev_nrm = []
        for base, loc, evsrc in ((0, q_loc, evQ), (2, k_loc, evK)):
            for s in range(2):
                ev_acc = None
                for ch in range(8):
                    deps_src = [evsrc[s][2 * ch], evsrc[s][2 * ch + 1]]
                    ev_s1 = P.add("act", ACTF(sqn_t, loc[:, s, ch * 1024:(ch + 1) * 1024],
                                              AF.Square),
                                  deps=[ev_sqn_use] + deps_src + ev_sqn_deps)
                    ev_s2 = P.add("dve", CP(sqn_b, sqn_t), deps=[ev_s1])
                    mmev = None
                    for hh in range(2):
                        mmev = P.add("pe", MM(psL[0:1, :], ones_col,
                                              sqn_b[:, hh * 512:(hh + 1) * 512], hh == 0, hh == 1),
                                     deps=[ev_s2, ev_on3, ev_psl_use])
                    ev_s3 = P.add("dve", CP(lrow, psL[0:1, :]), deps=[mmev])
                    ev_psl_use = ev_s3
                    ev_s4 = P.add("dve", RMAX(nrm[0:1, 6:7], lrow), deps=[ev_s3])
                    if ev_acc is None:
                        ev_acc = P.add("dve", CP(nrm[0:1, base + s:base + s + 1],
                                                 nrm[0:1, 6:7]), deps=[ev_s4])
                    else:
                        ev_acc = P.add("dve", TT(nrm[0:1, base + s:base + s + 1],
                                                 nrm[0:1, base + s:base + s + 1],
                                                 nrm[0:1, 6:7], ALU.max), deps=[ev_s4])
                    ev_sqn_use = ev_s2
                ev_nrm.append(ev_acc)
        ev_q1 = P.add("act", ACTF(nrm[0:1, 0:4], nrm[0:1, 0:4], AF.Sqrt), deps=ev_nrm)
        ev_q2 = P.add("dve", TT(nrm[0:1, 4:6], nrm[0:1, 0:2], nrm[0:1, 2:4], ALU.mult),
                      deps=[ev_q1])
        ev_mh = P.add("dve", TS(nrm[0:1, 4:6], nrm[0:1, 4:6], 1.02 * SCALE, ALU.mult,
                                3.0, ALU.add), deps=[ev_q2])
        ev_c2 = []
        for s in range(2):
            ev_c2.append(P.add("dve", TS(c2row[0:1, s, :], iota_r16, slopes_sb[0:1, s:s + 1],
                                         ALU.mult, nrm[0:1, 4 + s:5 + s], ALU.add),
                               deps=[ev_io4, ev_slo, ev_mh]))
        for sslot in range(2):
            for hhalf in range(2):
                ev_alibi[sslot] = P.add(
                    "dve", TS(a2[sslot][:, hhalf * 512:(hhalf + 1) * 512], iota_kq,
                              slopes_bc[:, sslot:sslot + 1], ALU.mult),
                    deps=[ev_io2, ev_sbc, ev_alibi[sslot], ev_mh])
        ev_btab_last = None
        ev_btr_use = None
        for s in range(2):
            for qb in range(16):
                ev_b1 = P.add("dve", TS(btr_row, iota_r64, slopes_sb[0:1, s:s + 1], ALU.mult,
                                        c2row[0:1, s, qb:qb + 1], ALU.subtract),
                              deps=[ev_io3, ev_c2[s], ev_btr_use])
                bank = psG[qb % 4]
                ev_b2 = P.add("pe", MM(bank[:, 0:64], ones_f32, btr_row, True, True),
                              deps=[ev_b1, ev_on2, psg_free[qb % 4]])
                ev_b3 = P.add("dve", CP(btab[:, s, qb, :], bank[:, 0:64]), deps=[ev_b2])
                psg_free[qb % 4] = ev_b3
                ev_btr_use = ev_b2
                ev_btab_last = ev_b3

        if PHASE_LIMIT <= 2:
            P.add("pool", lambda e: None, deps=[ev_btab_last] + ev_qdump_evs)
            _emit(P, block, sems)
            return nc

        ev_probs = [None] * NT
        ev_lgt_use = None
        ev_psl_rt = ev_psl_use
        for tt in range(NT):
            pslot = psL[:, (tt % 4) * 8:(tt % 4) * 8 + 8]
            ev_r1 = P.add("pe", TR(pslot, lg_sb[:, tt * 128:(tt + 1) * 128],
                                   id_f32[0:8, 0:8]),
                          deps=[ev_lg[tt], ev_idf, ev_psl_rt])
            ev_r2 = P.add("dve", CP(lgt_tm, pslot), deps=[ev_r1, ev_lgt_use])
            ev_psl_rt = ev_r2
            ev_r3 = P.add("dve", (lambda o=m8_t, i=lgt_tm: (lambda e: e.max(out=o, in_=i)))(),
                          deps=[ev_r2])
            ev_r4 = P.add("dve", TS(m8_t, lgt_tm, m8_t[:, 1:2], ALU.is_ge), deps=[ev_r3])
            ev_r5 = P.add("act", ACTF(sig_t, lgt_tm, AF.Sigmoid), deps=[ev_r2, ev_r4])
            ev_r6 = P.add("dve", TT(probs_tm[:, tt, :], m8_t, sig_t, ALU.mult),
                          deps=[ev_r4, ev_r5])
            ev_lgt_use = ev_r6
            ev_probs[tt] = ev_r6
        if "probs" in DBG:
            P.dma("sp", DMA(dbg["probs"], probs_tm.rearrange("p a b -> p (a b)")),
                  deps=[ev_probs[-1]])

        ev_stt_sl = [None, None]
        ev_exp_sl = [None, None]
        ev_pe_phat = [None, None, None, None]
        fin_free_av = [None, None]
        last_l_ev = ev_psl_rt
        ev_ats_use = [None]
        ev_psTf = [psT_free[3]]
        ev_atsb_use = [None]
        ev_agat = []
        pend = [None]
        ev_last_score = [None]

        def record_finalize(fs, fqb, fpar, fav, fsum):
            nonlocal last_l_ev
            q0 = fqb * 512
            e1 = P.add("dve", CP(lrow, psL[0:1, :]), deps=[fsum])
            last_l_ev = e1
            e_ev = P.add("act", ACP(ats_raw, psAV[fpar][:, :]), deps=[fav, ev_ats_use[0]])
            mtr = []
            for j in range(4):
                mtr.append(P.add("pe", MM(psAV[fpar][:, 2 * j:2 * j + 2],
                                          lrow[0:1, j * 128:(j + 1) * 128], id_f32[0:1, 0:2],
                                          True, True),
                                 deps=[e1, e_ev]))
            lview = psAV[fpar][:, 0:8].rearrange("p (a b) -> p a b", b=2)[:, :, 0:1]
            e_rc = P.add("dve", RECIP(rcol_sb, lview), deps=mtr)
            fin_free_av[fpar] = e_rc
            trs = []
            for j in range(4):
                trs.append(P.add("pe", TR(psT[:, j * 128:(j + 1) * 128],
                                          ats_raw[:, j * 128:(j + 1) * 128], id_bf),
                                 deps=[e_ev, ev_psTf[0]]))
            evs = []
            for j in range(4):
                evs.append(P.add("dve", TS(at_sb[:, j, :], psT[:, j * 128:(j + 1) * 128],
                                           rcol_sb[:, j:j + 1], ALU.mult),
                                 deps=[trs[j], e_rc, ev_atsb_use[0]]))
            ev_psTf[0] = evs[-1]
            ev_ats_use[0] = evs[-1]
            e7 = P.dma("sp", DMA(ag_at_in[q0:q0 + 512, fs * 128:(fs + 1) * 128]
                                 .rearrange("(a p) c -> p a c", p=128), at_sb), deps=evs)
            ev_atsb_use[0] = e7
            ev_agat.append(e7)

        delayed = []

        def flush_delayed():
            if not delayed:
                return
            (ds, dqb, dpar, dn, dt0, dtf, ds4, dexps) = delayed.pop(0)
            if dt0 == dtf and pend[0] is not None:
                args = pend[0]
                pend[0] = None
                record_finalize(*args)
            ev_av = ev_sum = None
            for hh in range(2):
                t = dt0 + hh
                dep_av = [dexps[hh], evV[t]]
                if t == dtf:
                    dep_av.append(fin_free_av[dpar])
                ev_av = P.add("pe", MM(psAV[dpar][:, :],
                                       v_loc[:, t, ds * 128:(ds + 1) * 128],
                                       phat[ds4][:, hh * 512:(hh + 1) * 512],
                                       t == dtf, t == dn - 1), deps=dep_av)
                dep_sum = [last_l_ev] if t == dtf else []
                ev_sum = P.add("pe", MM(psL[0:1, :], ones_col,
                                        phat[ds4][:, hh * 512:(hh + 1) * 512],
                                        t == dtf, t == dn - 1), deps=dep_sum)
            ev_pe_phat[ds4] = ev_sum
            if dt0 + 2 >= dn:
                pend[0] = (ds, dqb, dpar, ev_av, ev_sum)

        gstep = [0]
        SPAN0 = 20
        for s in range(2):
            for qb in range(16):
                n = 4 * qb + 4
                q0 = 512 * qb
                par = (s * 16 + qb) % 2
                tfirst = max(0, n - SPAN0) if s == 0 else 0
                for m in range(tfirst // 2, n // 2):
                    s2 = m % 2
                    s4 = gstep[0] % 4
                    gstep[0] += 1
                    t0t = 2 * m
                    scev = []
                    for hh in range(2):
                        t = t0t + hh
                        kdep = evK[s][t // 4]
                        qdep = evQ[s][qb] if (m == tfirst // 2 and hh == 0) else None
                        e_sc = P.add("pe", MM(psS[s2][:, hh * 512:(hh + 1) * 512],
                                              k_loc[:, s, t * 128:(t + 1) * 128],
                                              q_loc[:, s, q0:q0 + 512], True, True),
                                     deps=[ev_stt_sl[s2] if hh == 0 else None, kdep, qdep])
                        scev.append(e_sc)
                    ev_last_score[0] = scev[-1]
                    e_st = P.add("dve", STT(ssb[s2], psS[s2][:, :], SCALE, a2[s],
                                            ALU.mult, ALU.add),
                                 deps=[scev[-1], ev_exp_sl[s2], ev_alibi[s]])
                    ev_stt_sl[s2] = e_st
                    for hh in range(2):
                        t = t0t + hh
                        j = t - (n - 4)
                        if j >= 0:
                            e_st = P.add("dve", TT(ssb[s2][:, hh * 512:(hh + 1) * 512],
                                                   ssb[s2][:, hh * 512:(hh + 1) * 512],
                                                   m01[j], ALU.add),
                                         deps=[ev_m01[j], e_st])
                    exps = []
                    for hh in range(2):
                        t = t0t + hh
                        e_ex = P.add("act", ACTF(phat[s4][:, hh * 512:(hh + 1) * 512],
                                                 ssb[s2][:, hh * 512:(hh + 1) * 512], AF.Exp,
                                                 bias=btab[:, s, qb, t:t + 1], scale=1.0),
                                     deps=[e_st, ev_pe_phat[s4] if hh == 0 else None,
                                           ev_btab_last if (s == 0 and qb == 0
                                                            and m == tfirst // 2) else None])
                        exps.append(e_ex)
                    ev_exp_sl[s2] = exps[-1]
                    if len(delayed) >= 2:
                        flush_delayed()
                    delayed.append((s, qb, par, n, t0t, tfirst, s4, exps))
        while delayed:
            flush_delayed()
        record_finalize(*pend[0])
        pend[0] = None
        if "fl" in DBG:
            P.dma("sp", DMA(dbg["btab"], btab.rearrange("p a b c -> p (a b c)")),
                  deps=[ev_btab_last])
            P.dma("sp", DMA(dbg["a2"], a2[0]), deps=[ev_alibi[0]])
            P.dma("sp", DMA(dbg["lrow"], lrow), deps=[last_l_ev])
            P.dma("sp", DMA(dbg["rcol"], rcol_sb), deps=[fin_free_av[1]])
            P.dma("sp", DMA(dbg["atsr"], ats_raw), deps=[ev_ats_use[0]])
            P.dma("sp", DMA(dbg["atsb"], at_sb.rearrange("p a b -> p (a b)")),
                  deps=[ev_atsb_use[0]])
            P.dma("sp", DMA(dbg["phat"], phat[1]), deps=[ev_exp_sl[1], ev_pe_phat[1]])
            P.dma("sp", DMA(dbg["ssb"][:, 0:1024], ssb[0]), deps=[ev_stt_sl[0]])
            P.dma("sp", DMA(dbg["ssb"][:, 1024:2048], ssb[1]), deps=[ev_stt_sl[1]])
            P.dma("sp", DMA(dbg["nrm"], nrm), deps=[ev_mh])

        if "attnin" in DBG:
            P.dma("pool", DMA(dbg["attnin"], ag_at_in), deps=[ev_agat[-1]])

        if PHASE_LIMIT <= 3:
            ev_cc_at3 = P.cc(lambda e: e.collective_compute(
                "AllGather", ALU.bypass, replica_groups=[list(range(NCORE))],
                ins=[ag_at_in.opt()], outs=[ag_at_out.opt()]), deps=P.dma_all("sp"))
            if "attnin" in DBG:
                P.dma("pool", DMA(dbg["attnin"], ag_at_in), deps=[ev_agat[-1]])
            P.add("pool", lambda e: None, deps=[ev_cc_at3])
            _emit(P, block, sems)
            return nc


        EVB = P.now()
        ev_xn2 = P.dma("sp", DMA(xnfm2, ag_xn_in.rearrange("(a p) t -> p a t", p=128)),
                       deps=EVB)
        psg_free[0] = ev_stt_sl[0]
        psg_free[1] = ev_stt_sl[1]
        psg_free[2] = ev_stt_sl[0]
        psg_free[3] = ev_stt_sl[1]


        ev_wgu_ld = [None, None]
        ev_wgu_use = [ev_btab_last, ev_xng_use[1]]
        ev_wdn_ld = [None, None]
        ev_wdn_use = [ev_xng_use[1], ev_xng_use[1]]
        ev_comb_w = {}
        ev_he_w = [None] * 16
        ev_he_read = [None]
        ev_ast_w = [None] * 8
        ev_bscr = [None, None]
        ev_tmpf = [None, None]
        gucnt = [0]
        ev_pbc_use = [None]
        ev_pbc_read = [None]
        ev_ast_read = [None] * 8
        ev_prow_use = [None]
        n_pool_w = [0]
        ev_cc_at_box = [None]

        def load_gu(dram_slice, sl, extra=()):
            dep = [ev_wgu_use[sl]] + list(extra)
            ev_wgu_ld[sl] = P.dma("pool", DMA(wgu_sb[sl],
                                              dram_slice.rearrange("(a p) c -> p a c", p=128)),
                                  deps=dep)
            n_pool_w[0] += 1
            if n_pool_w[0] == 2:
                ev_cc_at_box[0] = P.cc(lambda e: e.collective_compute(
                    "AllGather", ALU.bypass, replica_groups=[list(range(NCORE))],
                    ins=[ag_at_in.opt()], outs=[ag_at_out.opt()]), deps=P.dma_all("sp"))
            return ev_wgu_ld[sl]

        def gate_up_block(wld, wbuf, icpair, is_gate, with_p):
            evs = []
            for fc in range(2):
                ic = icpair * 2 + fc
                for th in range(2):
                    gb = (fc * 2 + th) % 4
                    bank = psG[gb]
                    last = None
                    for dc in range(16):
                        dep0 = [wld, ev_xn2, psg_free[gb]] if dc == 0 else []
                        last = P.add("pe", MM(bank[:, :], wbuf[:, dc, fc * 128:(fc + 1) * 128],
                                              xnfm2[:, dc, th * 512:(th + 1) * 512],
                                              dc == 0, dc == 15), deps=dep0)
                    if is_gate:
                        sl4 = ic % 4
                        dst = a_stage[:, sl4, th * 512:(th + 1) * 512]
                        if with_p:
                            eva = P.add("dve", TT(dst, bank[:, :],
                                                  p_bc[:, th * 512:(th + 1) * 512], ALU.mult),
                                        deps=[last, ev_pbc_use[0], ev_ast_w[sl4 * 2 + th],
                                              ev_ast_read[sl4 * 2 + th]])
                            ev_pbc_read[0] = eva
                        else:
                            eva = P.add("dve", CP(dst, bank[:, :]),
                                        deps=[last, ev_ast_w[sl4 * 2 + th],
                                              ev_ast_read[sl4 * 2 + th]])
                        evb = P.add("act", ACTF(dst, dst, AF.Silu), deps=[eva])
                        ev_ast_w[sl4 * 2 + th] = evb
                        psg_free[gb] = evb
                        evs.append(evb)
                    else:
                        hdst = h_e[:, ic, th * 512:(th + 1) * 512]
                        sl4 = ic % 4
                        if with_p:
                            evb1 = P.add("dve", TT(hdst, bank[:, :],
                                                   p_bc[:, th * 512:(th + 1) * 512], ALU.mult),
                                         deps=[last, ev_pbc_use[0], ev_he_w[ic * 2 + th],
                                               ev_he_read[0]])
                            ev_pbc_read[0] = evb1
                        else:
                            evb1 = P.add("dve", CP(hdst, bank[:, :]),
                                         deps=[last, ev_he_w[ic * 2 + th], ev_he_read[0]])
                        evb2 = P.add("dve", TT(hdst,
                                               a_stage[:, sl4, th * 512:(th + 1) * 512],
                                               hdst, ALU.mult),
                                     deps=[evb1, ev_ast_w[sl4 * 2 + th]])
                        ev_ast_read[sl4 * 2 + th] = evb2
                        ev_he_w[ic * 2 + th] = evb2
                        psg_free[gb] = evb2
                        evs.append(evb2)
            return evs

        def down_pass(first, wdram, he_th):
            last_read = None
            for db in range(4):
                sl = db % 2
                ev_wdn_ld[sl] = P.dma("pool", DMA(wdn_sb[sl],
                                                  wdram[:, db * 512:(db + 1) * 512]
                                                  .rearrange("(a p) c -> p a c", p=128)),
                                      deps=[ev_wdn_use[sl], ev_xn2])
                lastu = None
                for tt in range(NT):
                    bank = psG[tt % 4]
                    last = None
                    for ic in range(8):
                        dep0 = ([ev_wdn_ld[sl], psg_free[tt % 4]] + he_th[tt // 4]) if ic == 0 else []
                        last = P.add("pe", MM(bank[:, :], h_e[:, ic, tt * 128:(tt + 1) * 128],
                                              wdn_sb[sl][:, ic, :], ic == 0, ic == 7),
                                     deps=dep0)
                    last_read = last
                    dst = comb[:, tt, db * 512:(db + 1) * 512]
                    key = (tt, db)
                    if first:
                        ev_c = P.add("dve", CP(dst, bank[:, :]), deps=[last])
                    else:
                        ev_c = P.add("dve", TT(dst, bank[:, :], dst, ALU.add),
                                     deps=[last, ev_comb_w.get(key)])
                    ev_comb_w[key] = ev_c
                    psg_free[tt % 4] = ev_c
                    lastu = ev_c
                ev_wdn_use[sl] = lastu
            ev_he_read[0] = last_read

        sh_th = [[], []]
        sh_plan = [(w_sg, 0, True, 0), (w_sg, 1, True, 1), (w_su, 0, False, 0),
                   (w_su, 1, False, 1), (w_sg, 2, True, 2), (w_sg, 3, True, 3),
                   (w_su, 2, False, 2), (w_su, 3, False, 3)]
        for i, (wsrc, blk, is_g, icp) in enumerate(sh_plan):
            sl = i % 2
            wld = load_gu(wsrc[:, blk * 256:(blk + 1) * 256], sl)
            evs = gate_up_block(wld, wgu_sb[sl], icp, is_g, False)
            ev_wgu_use[sl] = evs[-1]
            sh_th[0] += evs[0::2]
            sh_th[1] += evs[1::2]
        down_pass(True, w_sd, sh_th)

        for e in range(E):
            ev_prow = None
            for tt in range(NT):
                bank = psG[tt % 4]
                ev_p1 = P.add("pe", TR(bank[0:1, 0:128], probs_tm[:, tt, e:e + 1], id_f32),
                              deps=[ev_probs[tt], psg_free[tt % 4], ev_idf])
                ev_p2 = P.add("dve", CP(p_rows[0:1, tt * 128:(tt + 1) * 128], bank[0:1, 0:128]),
                              deps=[ev_p1, ev_prow_use[0]])
                psg_free[tt % 4] = ev_p2
                ev_prow = ev_p2
            ev_prow_use[0] = ev_prow
            evpb = None
            for th in range(2):
                bank = psG[th]
                ev_q1b = P.add("pe", MM(bank[:, :], ones_f32, p_rows[0:1, th * 512:(th + 1) * 512],
                                        True, True), deps=[ev_prow, psg_free[th], ev_on2])
                evpb = P.add("act", ACP(p_bc[:, th * 512:(th + 1) * 512], bank[:, :]),
                             deps=[ev_q1b, ev_pbc_use[0], ev_pbc_read[0]])
                psg_free[th] = evpb
            ev_pbc_use[0] = evpb
            blocks = [(0, True, 0), (1, True, 1), (4, False, 0), (5, False, 1),
                      (2, True, 2), (3, True, 3), (6, False, 2), (7, False, 3)]
            he_th = [[], []]
            for (blk, is_gate, icpair) in blocks:
                sl = blk % 2
                if blk < 4:
                    colbase = blk * 256
                else:
                    colbase = FI + (blk - 4) * 256
                wld = load_gu(w_gu[e, :, colbase:colbase + 256], sl)
                evs = gate_up_block(wld, wgu_sb[sl], icpair, is_gate, True)
                ev_wgu_use[sl] = evs[-1]
                he_th[0] += evs[0::2]
                he_th[1] += evs[1::2]
            down_pass(False, w_dn[e], he_th)

        if "comb" in DBG:
            P.dma("sp", DMA(dbg["comb"], comb.rearrange("p a b -> p (a b)")),
                  deps=[ev_comb_w[(tt, db)] for tt in range(NT) for db in range(4)])
        ev_g = []
        for tt in range(NT):
            deps = [ev_comb_w[(tt, db)] for db in range(4)]
            ev_g.append(P.add("act", ACTF(g_sb[:, tt, :], comb[:, tt, :], AF.Silu), deps=deps))

        if PHASE_LIMIT <= 4:
            P.add("pool", lambda e: None, deps=[ev_g[-1], ev_cc_at_box[0]])
            _emit(P, block, sems)
            return nc

        ev_gi = P.dma("pool", DMA(idxs_sb, gidx.rearrange("a p -> p a", p=128)))
        ev_gath = P.dma("pool", (lambda e: e.indirect_dma_start(
            out=gbuf, out_offset=None,
            in_=ag_at_out.rearrange("(b a) c -> b (a c)", a=64),
            in_offset=bass.IndirectOffsetOnAxis(ap=idxs_sb[:, 0:1], axis=0))),
            deps=[ev_cc_at_box[0], ev_gi] + ev_g)
        ev_cmp = P.dma("pool", DMA(at_compact.rearrange("(p a) c -> p a c", a=64),
                                   gbuf.rearrange("p (a c) -> p a c", c=256)),
                       deps=[ev_gath])
        ev_atfm_l = []
        for r in range(8):
            for s in range(2):
                chunk = r * 2 + s
                ev_atfm_l.append(P.dma("sp", (lambda o=atfm[:, chunk, :],
                                       i=at_compact[r * T:(r + 1) * T, s * 128:(s + 1) * 128]:
                                       (lambda e: e.dma_start_transpose(out=o, in_=i)))(),
                                deps=[ev_cmp] + ev_g))
        ev_atfm = ev_atfm_l[-8:]
        ev_bo = P.dma("pool", DMA(bout_sb, bout_d), deps=ev_g)

        ev_wo_ld = [None, None]
        ev_wo_use = [ev_g[-1], ev_g[-1]]
        ev_xsl = [None, None]
        ev_xsl_use = [None, None]
        ev_out_use = [None, None]
        ev_y = []
        for oc in range(4):
            wsl = oc % 2
            ev_wo_ld[wsl] = P.dma("pool", DMA(wout_sb[wsl],
                                              wout_t[:, oc * 512:(oc + 1) * 512]
                                              .rearrange("(a p) c -> p a c", p=128)),
                                  deps=[ev_wo_use[wsl], ev_cmp])
            lastu = None
            for tt in range(NT):
                k = oc * NT + tt
                xsl = k % 2
                ev_xsl[xsl] = P.dma("sp", DMA(xsl_t[xsl],
                                              x_my[tt * 128:(tt + 1) * 128,
                                                   oc * 512:(oc + 1) * 512]),
                                    deps=[ev_xsl_use[xsl]] + (ev_g if k < 2 else []))
                bank = psG[tt % 4]
                last = None
                for ch in range(16):
                    dep0 = ([ev_wo_ld[wsl], psg_free[tt % 4]] + ev_atfm) if ch == 0 else []
                    last = P.add("pe", MM(bank[:, :], atfm[:, ch, tt * 128:(tt + 1) * 128],
                                          wout_sb[wsl][:, ch, :], ch == 0, False), deps=dep0)
                last = P.add("pe", MM(bank[:, :], ones_bf[0:1, 0:128],
                                      bout_sb[0:1, oc * 512:(oc + 1) * 512], False, True),
                             deps=[ev_bo])
                lastu = last
                ev_o1 = P.add("dve", TT(tmp_f, bank[:, :], g_sb[:, tt, oc * 512:(oc + 1) * 512],
                                        ALU.mult), deps=[last, ev_g[tt], ev_tmpf[0]])
                ev_o2 = P.add("dve", TT(out_t[xsl], tmp_f, xsl_t[xsl], ALU.add),
                              deps=[ev_o1, ev_xsl[xsl], ev_out_use[xsl]])
                ev_tmpf[0] = ev_o2
                psg_free[tt % 4] = ev_o2
                ev_yy = P.dma("sp", DMA(y[tt * 128:(tt + 1) * 128, oc * 512:(oc + 1) * 512],
                                        out_t[xsl]), deps=[ev_o2])
                ev_out_use[xsl] = ev_yy
                ev_xsl_use[xsl] = ev_o2
                ev_y.append(ev_yy)
            ev_wo_use[wsl] = lastu

        P.add("pool", lambda e: None, deps=P.dma_all("sp"))

        _emit(P, block, sems)

    return nc


def _emit(P, block, sems):
    def mk(eng):
        def body(obj):
            P.emit_engine(eng, obj, sems)
        return body

    block.tensor(mk("pe"))
    block.scalar(mk("act"))
    block.vector(mk("dve"))
    block.gpsimd(mk("pool"))
    block.sync(mk("sp"))


_NC_CACHE = {}
_LAST_RES = None


def _prep_inputs(inputs):
    g = {k: np.asarray(v, dtype=np.float32) for k, v in inputs.items()}
    x = g["L0_attn_hidden_states"].reshape(S, D)
    qkv_w = g["L0_attn_qkv_w"]
    qkv_b = g["L0_attn_qkv_b"]
    out_w = g["L0_attn_out_w"]
    out_b = g["L0_attn_out_b"]
    slopes = g["L0_attn_alibi_slopes"]

    perm = []
    for r in range(8):
        for sl in range(2):
            h = 8 * sl + r
            perm.extend(range(h * 128, (h + 1) * 128))
    wout_t_perm = np.ascontiguousarray(out_w.T[perm, :])

    shared = {
        "w_router": np.ascontiguousarray(g["L0_ffn_router_w"]),
        "w_gu": np.ascontiguousarray(g["L0_ffn_gate_up_w"]),
        "w_dn": np.ascontiguousarray(g["L0_ffn_down_w"]),
        "w_sg": np.ascontiguousarray(g["L0_ffn_shared_gate_w"]),
        "w_su": np.ascontiguousarray(g["L0_ffn_shared_up_w"]),
        "w_sd": np.ascontiguousarray(g["L0_ffn_shared_down_w"]),
        "wout_t": wout_t_perm,
        "bout": np.ascontiguousarray(out_b.reshape(1, D)),
    }
    maps = []
    for c in range(NCORE):
        rows = np.r_[c * 128:(c + 1) * 128, (8 + c) * 128:(9 + c) * 128]
        m = dict(shared)
        m["x_my"] = np.ascontiguousarray(x[c * T:(c + 1) * T, :])
        m["wq_t"] = np.ascontiguousarray(qkv_w[rows, :].T)
        m["wk_t"] = np.ascontiguousarray(qkv_w[2048 + rows, :].T)
        m["wv_t"] = np.ascontiguousarray(qkv_w[4096 + rows, :].T)
        m["bq"] = np.ascontiguousarray(qkv_b[rows].reshape(1, 256))
        m["bk"] = np.ascontiguousarray(qkv_b[2048 + rows].reshape(1, 256))
        m["bv"] = np.ascontiguousarray(qkv_b[4096 + rows].reshape(1, 256))
        m["slopes2"] = np.ascontiguousarray(slopes[[c, 8 + c]].reshape(1, 2))
        gi = np.array([r * 128 + c * 16 + j for r in range(8) for j in range(16)],
                      dtype=np.uint32).reshape(1, 128)
        m["gidx"] = gi
        maps.append(m)
    return maps


def kernel(**inputs):
    key = ("main", tuple(sorted(DBG)), PHASE_LIMIT)
    if key not in _NC_CACHE:
        _NC_CACHE[key] = build_nc()
    nc = _NC_CACHE[key]
    maps = _prep_inputs(inputs)
    res = run_bass_kernel_spmd(nc, maps, core_ids=list(range(NCORE)))
    global _LAST_RES
    _LAST_RES = res
    out = np.concatenate([np.asarray(res.results[c]["y"]) for c in range(NCORE)], axis=0)
    return out.reshape(1, S, D).astype(np.float32)


if __name__ == "__main__":
    nc = build_nc()
    ni = sum(1 for _ in nc.all_instructions())
    print("built ok, instructions:", ni)


# version 21 (final)
# speedup vs baseline: 1.0020x; 1.0020x over previous
import numpy as np
import ml_dtypes
from contextlib import ExitStack

import concourse.bass as bass
import concourse.mybir as mybir
from concourse.bass_utils import run_bass_kernel_spmd

F32 = mybir.dt.float32
BF16 = mybir.dt.bfloat16
U8 = mybir.dt.uint8
U32 = mybir.dt.uint32
AF = mybir.ActivationFunctionType
ALU = mybir.AluOpType
AX = mybir.AxisListType

NCORE = 8
S = 8192
D = 2048
T = S // NCORE
NT = T // 128
E = 8
FI = 1024
SCALE = 1.0 / float(np.sqrt(128.0))
EPS = 1e-5
KB = 1024

ENGS = ("pe", "act", "dve", "pool", "sp")
NDMASEM = 8
SEMS = (("pe", "act", "dve", "pool", "cc")
        + tuple(f"poolq{i}" for i in range(NDMASEM))
        + tuple(f"sp{i}" for i in range(NDMASEM)))

DBG = set()
PHASE_LIMIT = 99


class Prog:
    def __init__(self):
        self.steps = {e: [] for e in ENGS}
        self.cnt = {s: 0 for s in SEMS}
        self.awaited = {s: set() for s in SEMS}
        self.dma_n = {"poolq": 0, "sp": 0}

    def add(self, eng, emit, deps=(), sem=None, inc=1, always_inc=False):
        sem = sem or eng
        dl = []
        for d in deps:
            if d is None:
                continue
            ps, c = d
            if c <= 0:
                continue
            dl.append(d)
            self.awaited[ps].add(c)
        self.cnt[sem] += inc
        self.steps[eng].append((dl, emit, sem, self.cnt[sem], always_inc, inc))
        return (sem, self.cnt[sem])

    def dma(self, eng, emit, deps=()):
        q = "poolq" if eng == "pool" else "sp"
        i = self.dma_n[q]
        self.dma_n[q] = i + 1
        sem = f"{q}{i % NDMASEM}"
        prior = (sem, 16 * (i // NDMASEM))
        return self.add(eng, emit, list(deps) + [prior], sem=sem, inc=16, always_inc=True)

    def dma_all(self, q):
        return [(f"{q}{j}", self.cnt[f"{q}{j}"]) for j in range(NDMASEM)]

    def cc(self, emit, deps=()):
        return self.add("pool", emit, deps, sem="cc", inc=1, always_inc=True)

    def now(self):
        return [(s, self.cnt[s]) for s in SEMS if self.cnt[s] > 0]

    def emit_engine(self, eng, obj, semobjs):
        waited = {}
        emitted = {s: 0 for s in SEMS}
        for (dl, emit, sem, cnt_after, always, inc) in self.steps[eng]:
            for (ps, c) in dl:
                if waited.get(ps, 0) >= c:
                    continue
                obj.wait_ge(semobjs[ps], c)
                waited[ps] = c
            inst = emit(obj)
            if always or (cnt_after in self.awaited[sem]):
                delta = cnt_after - emitted[sem]
                assert delta > 0
                if always:
                    assert delta == inc
                if inst is None:
                    obj.sem_inc(semobjs[sem], delta)
                else:
                    inst.then_inc(semobjs[sem], delta)
                emitted[sem] = cnt_after


def _dtsz(dt):
    return 2 if dt == BF16 else 4


class Arena:
    def __init__(self, ap, nbytes):
        self.ap = ap
        self.nbytes = nbytes

    def carve(self, off_kb, shape, dtype):
        off = int(round(off_kb * KB))
        part = int(shape[0])
        fsize = int(np.prod(shape[1:])) * _dtsz(dtype) if len(shape) > 1 else _dtsz(dtype)
        assert off % 32 == 0, off_kb
        assert off + fsize <= self.nbytes, (off_kb, shape, dtype)
        v = self.ap[0:part, off:off + fsize].bitcast(dtype)
        if len(shape) <= 2:
            return v
        nf = len(shape) - 1
        pat = "p (" + " ".join(f"d{i}" for i in range(nf)) + ") -> p " + \
            " ".join(f"d{i}" for i in range(nf))
        kw = {f"d{i}": int(shape[1 + i]) for i in range(nf)}
        return v.rearrange(pat, **kw)


def build_nc():
    nc = bass.Bass("TRN2", target_bir_lowering=False, debug=False)

    x_my = nc.dram_tensor("x_my", [T, D], F32, kind="ExternalInput").ap()
    wq_t = nc.dram_tensor("wq_t", [D, 256], F32, kind="ExternalInput").ap()
    wk_t = nc.dram_tensor("wk_t", [D, 256], F32, kind="ExternalInput").ap()
    wv_t = nc.dram_tensor("wv_t", [D, 256], F32, kind="ExternalInput").ap()
    bq_d = nc.dram_tensor("bq", [1, 256], F32, kind="ExternalInput").ap()
    bk_d = nc.dram_tensor("bk", [1, 256], F32, kind="ExternalInput").ap()
    bv_d = nc.dram_tensor("bv", [1, 256], F32, kind="ExternalInput").ap()
    slopes2 = nc.dram_tensor("slopes2", [1, 2], F32, kind="ExternalInput").ap()
    w_router = nc.dram_tensor("w_router", [D, 8], F32, kind="ExternalInput").ap()
    w_gu = nc.dram_tensor("w_gu", [E, D, 2 * FI], F32, kind="ExternalInput").ap()
    w_dn = nc.dram_tensor("w_dn", [E, FI, D], F32, kind="ExternalInput").ap()
    w_sg = nc.dram_tensor("w_sg", [D, FI], F32, kind="ExternalInput").ap()
    w_su = nc.dram_tensor("w_su", [D, FI], F32, kind="ExternalInput").ap()
    w_sd = nc.dram_tensor("w_sd", [FI, D], F32, kind="ExternalInput").ap()
    wout_t = nc.dram_tensor("wout_t", [D, D], F32, kind="ExternalInput").ap()
    bout_d = nc.dram_tensor("bout", [1, D], F32, kind="ExternalInput").ap()
    gidx = nc.dram_tensor("gidx", [1, 128], U32, kind="ExternalInput").ap()
    y = nc.dram_tensor("y", [T, D], F32, kind="ExternalOutput").ap()

    ag_xn_in = nc.dram_tensor("ag_xn_in", [D, T], BF16).ap()
    ag_xn_out = nc.dram_tensor("ag_xn_out", [NCORE * D, T], BF16, addr_space="Shared").ap()
    ag_at_in = nc.dram_tensor("ag_at_in", [S, 256], BF16).ap()
    ag_at_out = nc.dram_tensor("ag_at_out", [NCORE * S, 256], BF16, addr_space="Shared").ap()
    at_compact = nc.dram_tensor("at_compact", [S, 256], BF16).ap()

    dbg = {}
    if "xn" in DBG:
        dbg["xn"] = nc.dram_tensor("dbg_xn", [D, T], BF16, kind="ExternalOutput").ap()
    if "lg" in DBG:
        dbg["lg"] = nc.dram_tensor("dbg_lg", [8, T], F32, kind="ExternalOutput").ap()
    if "attnin" in DBG:
        dbg["attnin"] = nc.dram_tensor("dbg_attnin", [S, 256], BF16, kind="ExternalOutput").ap()
    if "fl" in DBG:
        dbg["btab"] = nc.dram_tensor("dbg_btab", [128, 2 * 16 * 64], F32, kind="ExternalOutput").ap()
        dbg["a2"] = nc.dram_tensor("dbg_a2", [128, 1024], F32, kind="ExternalOutput").ap()
        dbg["lrow"] = nc.dram_tensor("dbg_lrow", [1, 512], F32, kind="ExternalOutput").ap()
        dbg["rcol"] = nc.dram_tensor("dbg_rcol", [128, 4], F32, kind="ExternalOutput").ap()
        dbg["atsr"] = nc.dram_tensor("dbg_atsr", [128, 512], BF16, kind="ExternalOutput").ap()
        dbg["atsb"] = nc.dram_tensor("dbg_atsb", [128, 512], BF16, kind="ExternalOutput").ap()
        dbg["phat"] = nc.dram_tensor("dbg_phat", [128, 1024], BF16, kind="ExternalOutput").ap()
        dbg["ssb"] = nc.dram_tensor("dbg_ssb", [128, 4096], F32, kind="ExternalOutput").ap()
        dbg["nrm"] = nc.dram_tensor("dbg_nrm", [1, 8], F32, kind="ExternalOutput").ap()
    if "qkv" in DBG:
        dbg["q"] = nc.dram_tensor("dbg_q", [128, 2 * S], BF16, kind="ExternalOutput").ap()
        dbg["k"] = nc.dram_tensor("dbg_k", [128, 2 * S], BF16, kind="ExternalOutput").ap()
        dbg["v"] = nc.dram_tensor("dbg_v", [128, 64 * 256], BF16, kind="ExternalOutput").ap()
    if "probs" in DBG:
        dbg["probs"] = nc.dram_tensor("dbg_probs", [128, NT * 8], F32, kind="ExternalOutput").ap()
    if "comb" in DBG:
        dbg["comb"] = nc.dram_tensor("dbg_comb", [128, NT * D], F32, kind="ExternalOutput").ap()

    ARENA_KB = 176

    with ExitStack() as ctx:
        ec = ctx.enter_context
        arena_t = ec(nc.sbuf_tensor("arena", [128, ARENA_KB * KB], U8))
        A = Arena(arena_t.ap(), ARENA_KB * KB)

        psS = [ec(nc.psum_tensor("psS0", [128, 1024], F32)).ap(),
               ec(nc.psum_tensor("psS1", [128, 1024], F32)).ap()]
        psAV = [ec(nc.psum_tensor("psAV0", [128, 512], F32)).ap(),
                ec(nc.psum_tensor("psAV1", [128, 512], F32)).ap()]
        psL = ec(nc.psum_tensor("psL", [128, 512], F32)).ap()
        psG = [psS[0][:, 0:512], psS[1][:, 0:512],
               psS[0][:, 512:1024], psS[1][:, 512:1024]]
        psT = ec(nc.psum_tensor("psT", [128, 1024], BF16)).ap()

        sems = {s: ec(nc.semaphore("sem_" + s)) for s in SEMS}
        block = ec(nc.Block())
        P = Prog()

        xs_t = [A.carve(0, [128, D], F32), A.carve(8, [128, D], F32)]
        xnbf2 = [A.carve(16, [128, D], BF16), A.carve(20, [128, D], BF16)]
        lob2 = [A.carve(24, [128, D], BF16), A.carve(28, [128, D], BF16)]
        mvcol = A.carve(173.5625, [128, 12], F32)
        lo_scr = [A.carve(155, [128, 128], BF16), A.carve(155.25, [128, 128], BF16),
                  A.carve(155.5, [128, 128], BF16), A.carve(155.75, [128, 128], BF16)]
        xnfm = A.carve(32, [128, 16, T], BF16)
        q_loc = A.carve(0, [128, 2, S], BF16)
        k_loc = A.carve(32, [128, 2, S], BF16)
        v_loc = A.carve(64, [128, 64, 256], BF16)
        xng = [A.carve(96, [128, 16, 512], BF16), A.carve(112, [128, 16, 512], BF16)]
        wqkv_sb = {"q": A.carve(128, [128, 16, 256], BF16),
                   "k": A.carve(136, [128, 16, 256], BF16),
                   "v": A.carve(144, [128, 16, 256], BF16)}
        sqn_t = A.carve(96, [128, 1024], F32)
        sqn_b = A.carve(100, [128, 1024], BF16)
        btab = A.carve(128, [128, 2, 16, 64], F32)
        a2 = [A.carve(136, [128, 1024], F32), A.carve(140, [128, 1024], F32)]
        phat = [A.carve(144, [128, 1024], BF16), A.carve(146, [128, 1024], BF16),
                A.carve(148, [128, 1024], BF16), A.carve(150, [128, 1024], BF16)]
        ssb = [A.carve(120, [128, 1024], F32), A.carve(124, [128, 1024], F32)]
        ats_raw = A.carve(152, [128, 512], BF16)
        at_sb = A.carve(153, [128, 4, 128], BF16)
        lrow = A.carve(154, [1, 512], F32)
        rcol_sb = A.carve(154.0625, [128, 4], F32)
        xnfm2 = A.carve(0, [128, 16, T], BF16)
        comb = A.carve(32, [128, NT, D], F32)
        wgu_sb = [A.carve(96, [128, 16, 256], BF16), A.carve(104, [128, 16, 256], BF16)]
        wdn_sb = [A.carve(112, [128, 8, 512], BF16), A.carve(120, [128, 8, 512], BF16)]
        lgt_tm = A.carve(175.75, [128, 8], F32)
        m8_t = A.carve(175.78125, [128, 8], F32)
        sig_t = A.carve(175.8125, [128, 8], F32)
        tmp_f = A.carve(144, [128, 512], F32)
        p_bc = A.carve(128, [128, T], F32)
        p_rows = A.carve(167, [1, T], F32)
        h_e = A.carve(132, [128, 8, T], BF16)
        a_stage = A.carve(148, [128, 4, T], BF16)
        g_sb = A.carve(0, [128, NT, D], BF16)
        gbuf = A.carve(32, [128, 16384], BF16)
        atfm = A.carve(48, [128, 16, T], BF16)
        wout_sb = [A.carve(80, [128, 16, 512], BF16), A.carve(96, [128, 16, 512], BF16)]
        xsl_t = [A.carve(112, [128, 512], F32), A.carve(114, [128, 512], F32)]
        out_t = [A.carve(116, [128, 512], F32), A.carve(118, [128, 512], F32)]
        bout_sb = A.carve(148, [1, D], BF16)
        id_bf = A.carve(156, [128, 128], BF16)
        id_f32 = A.carve(156.5, [128, 128], F32)
        iota_kq = A.carve(157, [128, 512], F32)
        a_alibi = [A.carve(159, [128, 512], F32), A.carve(161, [128, 512], F32)]
        m01 = [A.carve(163 + 0.0, [128, 512], BF16), A.carve(164, [128, 512], BF16),
               A.carve(165, [128, 512], BF16), A.carve(166, [128, 512], BF16)]
        lg_sb = A.carve(167, [8, T], F32)
        probs_tm = A.carve(171, [128, NT, 8], F32)
        ones_bf = A.carve(171.25, [1, 512], BF16)
        ones_f32 = A.carve(172.25, [1, 128], F32)
        ones_col = A.carve(172.75, [128, 1], BF16)
        slopes_sb = A.carve(173, [1, 2], F32)
        slopes_bc = A.carve(173.03125, [128, 2], F32)
        iota_r64 = A.carve(173.0625, [1, 64], F32)
        iota_r16 = A.carve(173.3125, [1, 16], F32)
        c2row = A.carve(173.40625, [1, 2, 16], F32)
        nrm = A.carve(173.53125, [1, 8], F32)
        mcol = A.carve(173.5625, [128, 1], F32)
        vcol = A.carve(173.59375, [128, 1], F32)
        rstd = A.carve(173.625, [128, 1], F32)
        idxs_sb = A.carve(173.65625, [128, 1], U32)
        bq_sb = A.carve(173.75, [1, 256], BF16)
        bk_sb = A.carve(174.25, [1, 256], BF16)
        bv_sb = A.carve(174.75, [1, 256], BF16)
        wr_hi = A.carve(175.25, [128, 16, 8], BF16)
        wr_lo = A.carve(175.5, [128, 16, 8], BF16)
        wr_f32 = A.carve(152, [128, 16, 8], F32)
        btr_row = A.carve(150, [1, 64], F32)

        def MM(out, lhsT, rhs, start, stop):
            return lambda e: e.matmul(out=out, lhsT=lhsT, rhs=rhs, start=start, stop=stop)

        def TR(out, in_, ident):
            return lambda e: e.transpose(out=out, in_=in_, identity=ident)

        def CP(out, in_):
            return lambda e: e.tensor_copy(out=out, in_=in_)

        def ACP(out, in_):
            return lambda e: e.activation(out=out, in_=in_, func=AF.Copy)

        def ACTF(out, in_, func, **kw):
            return lambda e: e.activation(out=out, in_=in_, func=func, **kw)

        def TT(out, a, b, op):
            return lambda e: e.tensor_tensor(out=out, in0=a, in1=b, op=op)

        def TS(out, in0, s1, op0, s2=None, op1=None):
            if op1 is None:
                return lambda e: e.tensor_scalar(out=out, in0=in0, scalar1=s1, scalar2=None, op0=op0)
            return lambda e: e.tensor_scalar(out=out, in0=in0, scalar1=s1, scalar2=s2,
                                             op0=op0, op1=op1)

        def STT(out, in0, s, in1, op0, op1):
            return lambda e: e.scalar_tensor_tensor(out=out, in0=in0, scalar=s, in1=in1,
                                                    op0=op0, op1=op1)

        def RSUM(out, in_):
            return lambda e: e.reduce_sum(out=out, in_=in_, axis=AX.X)

        def RMAX(out, in_):
            return lambda e: e.reduce_max(out=out, in_=in_, axis=AX.X)

        def RECIP(out, in_):
            return lambda e: e.reciprocal(out=out, in_=in_)

        def DMA(out, in_):
            return lambda e: e.dma_start(out=out, in_=in_)

        ev_io1 = P.add("pool", lambda e: e.iota(out=id_f32, pattern=[[1, 128]], base=0,
                                                channel_multiplier=-1,
                                                allow_small_or_imprecise_dtypes=True))
        ev_idf = P.add("dve", TS(id_f32, id_f32, 0.0, ALU.is_equal), deps=[ev_io1])
        ev_idb = P.add("dve", CP(id_bf, id_f32), deps=[ev_idf])
        ev_io2 = P.add("pool", lambda e: e.iota(out=iota_kq, pattern=[[-1, 512]], base=0,
                                                channel_multiplier=1,
                                                allow_small_or_imprecise_dtypes=True))

        ev_io3 = P.add("pool", lambda e: e.iota(out=iota_r64, pattern=[[128, 64]], base=0,
                                                channel_multiplier=0,
                                                allow_small_or_imprecise_dtypes=True))
        ev_io4 = P.add("pool", lambda e: e.iota(out=iota_r16, pattern=[[512, 16]], base=0,
                                                channel_multiplier=0,
                                                allow_small_or_imprecise_dtypes=True))
        ev_on1 = P.add("pool", lambda e: e.memset(ones_bf, 1.0))
        ev_on2 = P.add("pool", lambda e: e.memset(ones_f32, 1.0))
        ev_on3 = P.add("pool", lambda e: e.memset(ones_col, 1.0))
        ev_m01 = [P.add("dve", TS(m01[j], iota_kq, float(-128 * j), ALU.is_gt,
                                  -1e9, ALU.mult), deps=[ev_io2])
                  for j in range(4)]
        ev_slo = P.dma("pool", DMA(slopes_sb, slopes2))
        ev_sb0 = P.add("pe", MM(psG[0][:, 0:2], ones_f32, slopes_sb, True, True),
                       deps=[ev_on2, ev_slo])
        ev_sbc = P.add("dve", CP(slopes_bc, psG[0][:, 0:2]), deps=[ev_sb0])
        ev_alibi = [None, None]
        psg_free = [ev_sbc, None, None, None]

        ev_bq = P.dma("pool", DMA(bq_sb, bq_d))
        ev_bk = P.dma("pool", DMA(bk_sb, bk_d))
        ev_bv = P.dma("pool", DMA(bv_sb, bv_d))
        ev_wrf = P.dma("pool", DMA(wr_f32, w_router.rearrange("(a p) c -> p a c", p=128)))
        ev_wrh = P.dma("pool", DMA(wr_hi, w_router.rearrange("(a p) c -> p a c", p=128)))
        ev_wrl = P.add("dve", STT(wr_lo, wr_f32, 1.0, wr_hi, ALU.mult, ALU.subtract),
                       deps=[ev_wrf, ev_wrh])

        ev_xld = [None] * 4
        ev_x_use = [None] * 4
        ev_xnbf_use = [None] * 4
        ev_lob_use = [None] * 4
        psT_free = [None] * 8
        ev_lg = [None] * NT
        ev_xn_w = None
        ev_loscr_use = [None, None, None, None]
        xs4 = [A.carve(0, [128, D], F32), A.carve(8, [128, D], F32),
               A.carve(16, [128, D], F32), A.carve(24, [128, D], F32)]
        xnbf4 = [A.carve(64, [128, D], BF16), A.carve(68, [128, D], BF16),
                 A.carve(72, [128, D], BF16), A.carve(76, [128, D], BF16)]
        lob4 = [A.carve(80, [128, D], BF16), A.carve(84, [128, D], BF16),
                A.carve(88, [128, D], BF16), A.carve(92, [128, D], BF16)]
        ev_sq4 = [None] * 4
        ev_lo4 = [None] * 4
        ev_xnb4 = [None] * 4

        def ln_a(tt):
            sl = tt % 4
            mcol2 = mvcol[:, sl:sl + 1]
            vcol2 = mvcol[:, 4 + sl:5 + sl]
            ev_xld[sl] = P.dma("sp", DMA(xs4[sl], x_my[tt * 128:(tt + 1) * 128, :]),
                               deps=[ev_x_use[sl]])
            ev_rs = P.add("dve", RSUM(mcol2, xs4[sl]), deps=[ev_xld[sl]])
            ev_m = P.add("dve", TS(mcol2, mcol2, 1.0 / D, ALU.mult), deps=[ev_rs])
            ev_xc = P.add("dve", TS(xs4[sl], xs4[sl], mcol2, ALU.subtract), deps=[ev_m])
            ev_sq4[sl] = P.add("act", ACTF(xnbf4[sl], xs4[sl], AF.Square, accum_out=vcol2),
                               deps=[ev_xc, ev_xnbf_use[sl]])

        def ln_b(tt):
            sl = tt % 4
            vcol2 = mvcol[:, 4 + sl:5 + sl]
            rcol2 = mvcol[:, 8 + sl:9 + sl]
            ev_var = P.add("dve", TS(vcol2, vcol2, 1.0 / D, ALU.mult, EPS, ALU.add),
                           deps=[ev_sq4[sl]])
            ev_sd = P.add("act", ACTF(vcol2, vcol2, AF.Sqrt), deps=[ev_var])
            ev_rst = P.add("dve", RECIP(rcol2, vcol2), deps=[ev_sd])
            ev_xnb4[sl] = P.add("dve", TS(xnbf4[sl], xs4[sl], rcol2, ALU.mult),
                                deps=[ev_rst, ev_sq4[sl]])
            ev_lo4[sl] = P.add("dve", STT(lob4[sl], xs4[sl], rcol2, xnbf4[sl],
                                          ALU.mult, ALU.subtract),
                               deps=[ev_xnb4[sl], ev_lob_use[sl]])
            ev_x_use[sl] = ev_lo4[sl]

        def tr_c(tt):
            nonlocal ev_xn_w
            sl = tt % 4
            last_mm = None
            last_trh = None
            last_trl = None
            for dc in range(16):
                sh = dc % 4
                slo = 4 + dc % 4
                pT_hi = psT[:, sh * 128:(sh + 1) * 128]
                pT_lo = psT[:, slo * 128:(slo + 1) * 128]
                ev_t1 = P.add("pe", TR(pT_hi, xnbf4[sl][:, dc * 128:(dc + 1) * 128], id_bf),
                              deps=[ev_xnb4[sl], ev_idb, psT_free[sh]])
                last_trh = ev_t1
                ev_e1 = P.add("dve", CP(xnfm[:, dc, tt * 128:(tt + 1) * 128], pT_hi),
                              deps=[ev_t1])
                psT_free[sh] = ev_e1
                ev_xn_w = ev_e1
                ev_t2 = P.add("pe", TR(pT_lo, lob4[sl][:, dc * 128:(dc + 1) * 128], id_bf),
                              deps=[ev_lo4[sl], ev_idb, psT_free[slo]])
                last_trl = ev_t2
                losl = dc % 4
                ev_e2 = P.add("act", ACP(lo_scr[losl], pT_lo),
                              deps=[ev_t2, ev_loscr_use[losl]])
                psT_free[slo] = ev_e2
                xnfm_chunk = xnfm[:, dc, tt * 128:(tt + 1) * 128]
                gbank = psG[tt % 4]
                m1 = P.add("pe", MM(gbank[0:8, 0:128], wr_hi[:, dc, :], xnfm_chunk,
                                    dc == 0, False),
                           deps=[ev_e1, ev_wrh, psg_free[tt % 4] if dc == 0 else None])
                m2 = P.add("pe", MM(gbank[0:8, 0:128], wr_lo[:, dc, :], xnfm_chunk,
                                    False, False), deps=[ev_wrl])
                m3 = P.add("pe", MM(gbank[0:8, 0:128], wr_hi[:, dc, :], lo_scr[losl],
                                    False, dc == 15), deps=[ev_e2])
                ev_loscr_use[losl] = m3
                last_mm = m3
            ev_lgv = P.add("dve", CP(lg_sb[:, tt * 128:(tt + 1) * 128],
                                     psG[tt % 4][0:8, 0:128]), deps=[last_mm])
            ev_lg[tt] = ev_lgv
            psg_free[tt % 4] = ev_lgv
            ev_xnbf_use[sl] = last_trh
            ev_lob_use[sl] = last_trl

        for tt in range(NT):
            ln_a(tt)
            if tt >= 1:
                ln_b(tt - 1)
            if tt >= 2:
                tr_c(tt - 2)
        ln_b(NT - 1)
        tr_c(NT - 2)
        tr_c(NT - 1)

        ev_xn_dma = P.dma("sp", DMA(ag_xn_in.rearrange("(a p) t -> p a t", p=128), xnfm),
                          deps=[ev_xn_w])
        ev_cc_xn = P.cc(lambda e: e.collective_compute(
            "AllGather", ALU.bypass, replica_groups=[list(range(NCORE))],
            ins=[ag_xn_in.opt()], outs=[ag_xn_out.opt()]), deps=[ev_xn_dma])
        if "xn" in DBG:
            P.dma("sp", DMA(dbg["xn"].rearrange("(a p) t -> p a t", p=128), xnfm),
                  deps=[ev_xn_w])
        finish_evs = [ev_cc_xn]
        if "lg" in DBG:
            P.dma("sp", DMA(dbg["lg"], lg_sb), deps=[ev_lg[NT - 1]])

        if PHASE_LIMIT <= 1:
            P.add("pool", lambda e: None, deps=finish_evs + [ev_lg[NT - 1]])
            _emit(P, block, sems)
            return nc

        ev_wq = P.dma("pool", DMA(wqkv_sb["q"], wq_t.rearrange("(a p) c -> p a c", p=128)))
        ev_wk = P.dma("pool", DMA(wqkv_sb["k"], wk_t.rearrange("(a p) c -> p a c", p=128)))
        ev_wv = P.dma("pool", DMA(wqkv_sb["v"], wv_t.rearrange("(a p) c -> p a c", p=128)))

        ev_xng = [None, None]
        ev_xng_use = [None, None]
        evQ = [[None] * 16, [None] * 16]
        evK = [[None] * 16, [None] * 16]
        evV = [None] * 64
        qk_plan = [("q", 0, bq_sb), ("q", 1, bq_sb), ("k", 0, bk_sb), ("k", 1, bk_sb)]
        for g in range(16):
            sl = g % 2
            r = g // 2
            cb = (g % 2) * 512
            r = g // 2
            r = (g * 512) // T
            cb = (g * 512) % T
            ev_xng[sl] = P.dma("sp", DMA(xng[sl],
                                         ag_xn_out[r * D:(r + 1) * D, cb:cb + 512]
                                         .rearrange("(a p) t -> p a t", p=128)),
                               deps=[ev_xng_use[sl], ev_cc_xn])
            last_use = None
            for i, (nm, s, brow) in enumerate(qk_plan):
                bank = psG[i % 4]
                last = None
                for dc in range(16):
                    dep0 = [ev_xng[sl], ev_wq, ev_wk, psg_free[i % 4]] if dc == 0 else []
                    last = P.add("pe", MM(bank[:, :], wqkv_sb[nm][:, dc, s * 128:(s + 1) * 128],
                                          xng[sl][:, dc, :], dc == 0, False), deps=dep0)
                last = P.add("pe", MM(bank[:, :], brow[0:1, s * 128:(s + 1) * 128], ones_bf,
                                      False, True), deps=[ev_bq, ev_bk, ev_on1])
                dst = (q_loc if nm == "q" else k_loc)[:, s, g * 512:(g + 1) * 512]
                if i % 2 == 0:
                    ev_ev = P.add("dve", CP(dst, bank[:, :]), deps=[last])
                else:
                    ev_ev = P.add("act", ACP(dst, bank[:, :]), deps=[last])
                psg_free[i % 4] = ev_ev
                if nm == "q":
                    evQ[s][g] = ev_ev
                else:
                    evK[s][g] = ev_ev
                last_use = ev_ev
            for vt in range(4):
                bank = psG[vt % 4]
                last = None
                for dc in range(16):
                    dep0 = [ev_xng[sl], ev_wv, psg_free[vt % 4]] if dc == 0 else []
                    last = P.add("pe", MM(bank[:, 0:256], xng[sl][:, dc, vt * 128:(vt + 1) * 128],
                                          wqkv_sb["v"][:, dc, :], dc == 0, False), deps=dep0)
                last = P.add("pe", MM(bank[:, 0:256], ones_bf[0:1, 0:128], bv_sb, False, True),
                             deps=[ev_bv])
                dst = v_loc[:, g * 4 + vt, :]
                if vt % 2 == 0:
                    ev_ev = P.add("dve", CP(dst, bank[:, 0:256]), deps=[last])
                else:
                    ev_ev = P.add("act", ACP(dst, bank[:, 0:256]), deps=[last])
                psg_free[vt % 4] = ev_ev
                evV[g * 4 + vt] = ev_ev
                last_use = ev_ev
            ev_xng_use[sl] = last_use

        ev_qdump_evs = []
        if "qkv" in DBG:
            ev_qdump_evs.append(P.dma("sp", DMA(dbg["q"], q_loc.rearrange("p a t -> p (a t)")),
                                      deps=[evQ[1][15]]))
            ev_qdump_evs.append(P.dma("sp", DMA(dbg["k"], k_loc.rearrange("p a t -> p (a t)")),
                                      deps=[evK[1][15]]))
            ev_qdump_evs.append(P.dma("sp", DMA(dbg["v"], v_loc.rearrange("p a t -> p (a t)")),
                                      deps=[evV[63]]))

        ev_sqn_deps = [ev_xng_use[0], ev_xng_use[1]]
        ev_sqn_use = None
        ev_psl_use = None
        ev_nrm = []
        for base, loc, evsrc in ((0, q_loc, evQ), (2, k_loc, evK)):
            for s in range(2):
                ev_acc = None
                for ch in range(8):
                    deps_src = [evsrc[s][2 * ch], evsrc[s][2 * ch + 1]]
                    ev_s1 = P.add("act", ACTF(sqn_t, loc[:, s, ch * 1024:(ch + 1) * 1024],
                                              AF.Square),
                                  deps=[ev_sqn_use] + deps_src + ev_sqn_deps)
                    ev_s2 = P.add("dve", CP(sqn_b, sqn_t), deps=[ev_s1])
                    mmev = None
                    for hh in range(2):
                        mmev = P.add("pe", MM(psL[0:1, :], ones_col,
                                              sqn_b[:, hh * 512:(hh + 1) * 512], hh == 0, hh == 1),
                                     deps=[ev_s2, ev_on3, ev_psl_use])
                    ev_s3 = P.add("dve", CP(lrow, psL[0:1, :]), deps=[mmev])
                    ev_psl_use = ev_s3
                    ev_s4 = P.add("dve", RMAX(nrm[0:1, 6:7], lrow), deps=[ev_s3])
                    if ev_acc is None:
                        ev_acc = P.add("dve", CP(nrm[0:1, base + s:base + s + 1],
                                                 nrm[0:1, 6:7]), deps=[ev_s4])
                    else:
                        ev_acc = P.add("dve", TT(nrm[0:1, base + s:base + s + 1],
                                                 nrm[0:1, base + s:base + s + 1],
                                                 nrm[0:1, 6:7], ALU.max), deps=[ev_s4])
                    ev_sqn_use = ev_s2
                ev_nrm.append(ev_acc)
        ev_q1 = P.add("act", ACTF(nrm[0:1, 0:4], nrm[0:1, 0:4], AF.Sqrt), deps=ev_nrm)
        ev_q2 = P.add("dve", TT(nrm[0:1, 4:6], nrm[0:1, 0:2], nrm[0:1, 2:4], ALU.mult),
                      deps=[ev_q1])
        ev_mh = P.add("dve", TS(nrm[0:1, 4:6], nrm[0:1, 4:6], 1.02 * SCALE, ALU.mult,
                                3.0, ALU.add), deps=[ev_q2])
        ev_c2 = []
        for s in range(2):
            ev_c2.append(P.add("dve", TS(c2row[0:1, s, :], iota_r16, slopes_sb[0:1, s:s + 1],
                                         ALU.mult, nrm[0:1, 4 + s:5 + s], ALU.add),
                               deps=[ev_io4, ev_slo, ev_mh]))
        for sslot in range(2):
            for hhalf in range(2):
                ev_alibi[sslot] = P.add(
                    "dve", TS(a2[sslot][:, hhalf * 512:(hhalf + 1) * 512], iota_kq,
                              slopes_bc[:, sslot:sslot + 1], ALU.mult),
                    deps=[ev_io2, ev_sbc, ev_alibi[sslot], ev_mh])
        ev_btab_last = None
        ev_btr_use = None
        for s in range(2):
            for qb in range(16):
                ev_b1 = P.add("dve", TS(btr_row, iota_r64, slopes_sb[0:1, s:s + 1], ALU.mult,
                                        c2row[0:1, s, qb:qb + 1], ALU.subtract),
                              deps=[ev_io3, ev_c2[s], ev_btr_use])
                bank = psG[qb % 4]
                ev_b2 = P.add("pe", MM(bank[:, 0:64], ones_f32, btr_row, True, True),
                              deps=[ev_b1, ev_on2, psg_free[qb % 4]])
                ev_b3 = P.add("dve", CP(btab[:, s, qb, :], bank[:, 0:64]), deps=[ev_b2])
                psg_free[qb % 4] = ev_b3
                ev_btr_use = ev_b2
                ev_btab_last = ev_b3

        if PHASE_LIMIT <= 2:
            P.add("pool", lambda e: None, deps=[ev_btab_last] + ev_qdump_evs)
            _emit(P, block, sems)
            return nc

        ev_probs = [None] * NT
        ev_lgt_use = None
        ev_psl_rt = ev_psl_use
        for tt in range(NT):
            pslot = psL[:, (tt % 4) * 8:(tt % 4) * 8 + 8]
            ev_r1 = P.add("pe", TR(pslot, lg_sb[:, tt * 128:(tt + 1) * 128],
                                   id_f32[0:8, 0:8]),
                          deps=[ev_lg[tt], ev_idf, ev_psl_rt])
            ev_r2 = P.add("dve", CP(lgt_tm, pslot), deps=[ev_r1, ev_lgt_use])
            ev_psl_rt = ev_r2
            ev_r3 = P.add("dve", (lambda o=m8_t, i=lgt_tm: (lambda e: e.max(out=o, in_=i)))(),
                          deps=[ev_r2])
            ev_r4 = P.add("dve", TS(m8_t, lgt_tm, m8_t[:, 1:2], ALU.is_ge), deps=[ev_r3])
            ev_r5 = P.add("act", ACTF(sig_t, lgt_tm, AF.Sigmoid), deps=[ev_r2, ev_r4])
            ev_r6 = P.add("dve", TT(probs_tm[:, tt, :], m8_t, sig_t, ALU.mult),
                          deps=[ev_r4, ev_r5])
            ev_lgt_use = ev_r6
            ev_probs[tt] = ev_r6
        if "probs" in DBG:
            P.dma("sp", DMA(dbg["probs"], probs_tm.rearrange("p a b -> p (a b)")),
                  deps=[ev_probs[-1]])

        ev_stt_sl = [None, None]
        ev_exp_sl = [None, None]
        ev_pe_phat = [None, None, None, None]
        fin_free_av = [None, None]
        last_l_ev = ev_psl_rt
        ev_ats_use = [None]
        ev_psTf = [psT_free[3]]
        ev_atsb_use = [None]
        ev_agat = []
        pend = [None]
        ev_last_score = [None]

        def record_finalize(fs, fqb, fpar, fav, fsum):
            nonlocal last_l_ev
            q0 = fqb * 512
            e1 = P.add("dve", CP(lrow, psL[0:1, :]), deps=[fsum])
            last_l_ev = e1
            e_ev = P.add("act", ACP(ats_raw, psAV[fpar][:, :]), deps=[fav, ev_ats_use[0]])
            mtr = []
            for j in range(4):
                mtr.append(P.add("pe", MM(psAV[fpar][:, 2 * j:2 * j + 2],
                                          lrow[0:1, j * 128:(j + 1) * 128], id_f32[0:1, 0:2],
                                          True, True),
                                 deps=[e1, e_ev]))
            lview = psAV[fpar][:, 0:8].rearrange("p (a b) -> p a b", b=2)[:, :, 0:1]
            e_rc = P.add("dve", RECIP(rcol_sb, lview), deps=mtr)
            fin_free_av[fpar] = e_rc
            trs = []
            for j in range(4):
                trs.append(P.add("pe", TR(psT[:, j * 128:(j + 1) * 128],
                                          ats_raw[:, j * 128:(j + 1) * 128], id_bf),
                                 deps=[e_ev, ev_psTf[0]]))
            evs = []
            for j in range(4):
                evs.append(P.add("dve", TS(at_sb[:, j, :], psT[:, j * 128:(j + 1) * 128],
                                           rcol_sb[:, j:j + 1], ALU.mult),
                                 deps=[trs[j], e_rc, ev_atsb_use[0]]))
            ev_psTf[0] = evs[-1]
            ev_ats_use[0] = evs[-1]
            e7 = P.dma("sp", DMA(ag_at_in[q0:q0 + 512, fs * 128:(fs + 1) * 128]
                                 .rearrange("(a p) c -> p a c", p=128), at_sb), deps=evs)
            ev_atsb_use[0] = e7
            ev_agat.append(e7)

        delayed = []

        def flush_delayed():
            if not delayed:
                return
            (ds, dqb, dpar, dn, dt0, dtf, ds4, dexps) = delayed.pop(0)
            if dt0 == dtf and pend[0] is not None:
                args = pend[0]
                pend[0] = None
                record_finalize(*args)
            ev_av = ev_sum = None
            for hh in range(2):
                t = dt0 + hh
                dep_av = [dexps[hh], evV[t]]
                if t == dtf:
                    dep_av.append(fin_free_av[dpar])
                ev_av = P.add("pe", MM(psAV[dpar][:, :],
                                       v_loc[:, t, ds * 128:(ds + 1) * 128],
                                       phat[ds4][:, hh * 512:(hh + 1) * 512],
                                       t == dtf, t == dn - 1), deps=dep_av)
                dep_sum = [last_l_ev] if t == dtf else []
                ev_sum = P.add("pe", MM(psL[0:1, :], ones_col,
                                        phat[ds4][:, hh * 512:(hh + 1) * 512],
                                        t == dtf, t == dn - 1), deps=dep_sum)
            ev_pe_phat[ds4] = ev_sum
            if dt0 + 2 >= dn:
                pend[0] = (ds, dqb, dpar, ev_av, ev_sum)

        gstep = [0]
        SPAN0 = 20
        for s in range(2):
            for qb in range(16):
                n = 4 * qb + 4
                q0 = 512 * qb
                par = (s * 16 + qb) % 2
                tfirst = max(0, n - SPAN0) if s == 0 else 0
                for m in range(tfirst // 2, n // 2):
                    s2 = m % 2
                    s4 = gstep[0] % 4
                    gstep[0] += 1
                    t0t = 2 * m
                    scev = []
                    for hh in range(2):
                        t = t0t + hh
                        kdep = evK[s][t // 4]
                        qdep = evQ[s][qb] if (m == tfirst // 2 and hh == 0) else None
                        e_sc = P.add("pe", MM(psS[s2][:, hh * 512:(hh + 1) * 512],
                                              k_loc[:, s, t * 128:(t + 1) * 128],
                                              q_loc[:, s, q0:q0 + 512], True, True),
                                     deps=[ev_stt_sl[s2] if hh == 0 else None, kdep, qdep])
                        scev.append(e_sc)
                    ev_last_score[0] = scev[-1]
                    e_st = P.add("dve", STT(ssb[s2], psS[s2][:, :], SCALE, a2[s],
                                            ALU.mult, ALU.add),
                                 deps=[scev[-1], ev_exp_sl[s2], ev_alibi[s]])
                    ev_stt_sl[s2] = e_st
                    for hh in range(2):
                        t = t0t + hh
                        j = t - (n - 4)
                        if j >= 0:
                            e_st = P.add("dve", TT(ssb[s2][:, hh * 512:(hh + 1) * 512],
                                                   ssb[s2][:, hh * 512:(hh + 1) * 512],
                                                   m01[j], ALU.add),
                                         deps=[ev_m01[j], e_st])
                    exps = []
                    for hh in range(2):
                        t = t0t + hh
                        e_ex = P.add("act", ACTF(phat[s4][:, hh * 512:(hh + 1) * 512],
                                                 ssb[s2][:, hh * 512:(hh + 1) * 512], AF.Exp,
                                                 bias=btab[:, s, qb, t:t + 1], scale=1.0),
                                     deps=[e_st, ev_pe_phat[s4] if hh == 0 else None,
                                           ev_btab_last if (s == 0 and qb == 0
                                                            and m == tfirst // 2) else None])
                        exps.append(e_ex)
                    ev_exp_sl[s2] = exps[-1]
                    if len(delayed) >= 2:
                        flush_delayed()
                    delayed.append((s, qb, par, n, t0t, tfirst, s4, exps))
        while delayed:
            flush_delayed()
        record_finalize(*pend[0])
        pend[0] = None
        if "fl" in DBG:
            P.dma("sp", DMA(dbg["btab"], btab.rearrange("p a b c -> p (a b c)")),
                  deps=[ev_btab_last])
            P.dma("sp", DMA(dbg["a2"], a2[0]), deps=[ev_alibi[0]])
            P.dma("sp", DMA(dbg["lrow"], lrow), deps=[last_l_ev])
            P.dma("sp", DMA(dbg["rcol"], rcol_sb), deps=[fin_free_av[1]])
            P.dma("sp", DMA(dbg["atsr"], ats_raw), deps=[ev_ats_use[0]])
            P.dma("sp", DMA(dbg["atsb"], at_sb.rearrange("p a b -> p (a b)")),
                  deps=[ev_atsb_use[0]])
            P.dma("sp", DMA(dbg["phat"], phat[1]), deps=[ev_exp_sl[1], ev_pe_phat[1]])
            P.dma("sp", DMA(dbg["ssb"][:, 0:1024], ssb[0]), deps=[ev_stt_sl[0]])
            P.dma("sp", DMA(dbg["ssb"][:, 1024:2048], ssb[1]), deps=[ev_stt_sl[1]])
            P.dma("sp", DMA(dbg["nrm"], nrm), deps=[ev_mh])

        if "attnin" in DBG:
            P.dma("pool", DMA(dbg["attnin"], ag_at_in), deps=[ev_agat[-1]])

        if PHASE_LIMIT <= 3:
            ev_cc_at3 = P.cc(lambda e: e.collective_compute(
                "AllGather", ALU.bypass, replica_groups=[list(range(NCORE))],
                ins=[ag_at_in.opt()], outs=[ag_at_out.opt()]), deps=P.dma_all("sp"))
            if "attnin" in DBG:
                P.dma("pool", DMA(dbg["attnin"], ag_at_in), deps=[ev_agat[-1]])
            P.add("pool", lambda e: None, deps=[ev_cc_at3])
            _emit(P, block, sems)
            return nc


        EVB = P.now()
        ev_xn2 = P.dma("sp", DMA(xnfm2, ag_xn_in.rearrange("(a p) t -> p a t", p=128)),
                       deps=EVB)
        psg_free[0] = ev_stt_sl[0]
        psg_free[1] = ev_stt_sl[1]
        psg_free[2] = ev_stt_sl[0]
        psg_free[3] = ev_stt_sl[1]


        ev_wgu_ld = [None, None]
        ev_wgu_use = [ev_btab_last, ev_xng_use[1]]
        ev_wdn_ld = [None, None]
        ev_wdn_use = [ev_xng_use[1], ev_xng_use[1]]
        ev_comb_w = {}
        ev_he_w = [None] * 16
        ev_he_read = [None]
        ev_ast_w = [None] * 8
        ev_bscr = [None, None]
        ev_tmpf = [None, None]
        gucnt = [0]
        ev_pbc_use = [None]
        ev_pbc_read = [None]
        ev_ast_read = [None] * 8
        ev_prow_use = [None]
        n_pool_w = [0]
        ev_cc_at_box = [None]

        def load_gu(dram_slice, sl, extra=()):
            dep = [ev_wgu_use[sl]] + list(extra)
            ev_wgu_ld[sl] = P.dma("pool", DMA(wgu_sb[sl],
                                              dram_slice.rearrange("(a p) c -> p a c", p=128)),
                                  deps=dep)
            n_pool_w[0] += 1
            if n_pool_w[0] == 2:
                ev_cc_at_box[0] = P.cc(lambda e: e.collective_compute(
                    "AllGather", ALU.bypass, replica_groups=[list(range(NCORE))],
                    ins=[ag_at_in.opt()], outs=[ag_at_out.opt()]), deps=P.dma_all("sp"))
            return ev_wgu_ld[sl]

        def gate_up_block(wld, wbuf, icpair, is_gate, with_p):
            evs = []
            for fc in range(2):
                ic = icpair * 2 + fc
                for th in range(2):
                    gb = (fc * 2 + th) % 4
                    bank = psG[gb]
                    last = None
                    for dc in range(16):
                        dep0 = [wld, ev_xn2, psg_free[gb]] if dc == 0 else []
                        last = P.add("pe", MM(bank[:, :], wbuf[:, dc, fc * 128:(fc + 1) * 128],
                                              xnfm2[:, dc, th * 512:(th + 1) * 512],
                                              dc == 0, dc == 15), deps=dep0)
                    if is_gate:
                        sl4 = ic % 4
                        dst = a_stage[:, sl4, th * 512:(th + 1) * 512]
                        if with_p:
                            eva = P.add("dve", TT(dst, bank[:, :],
                                                  p_bc[:, th * 512:(th + 1) * 512], ALU.mult),
                                        deps=[last, ev_pbc_use[0], ev_ast_w[sl4 * 2 + th],
                                              ev_ast_read[sl4 * 2 + th]])
                            ev_pbc_read[0] = eva
                        else:
                            eva = P.add("dve", CP(dst, bank[:, :]),
                                        deps=[last, ev_ast_w[sl4 * 2 + th],
                                              ev_ast_read[sl4 * 2 + th]])
                        evb = P.add("act", ACTF(dst, dst, AF.Silu), deps=[eva])
                        ev_ast_w[sl4 * 2 + th] = evb
                        psg_free[gb] = eva
                        evs.append(evb)
                    else:
                        hdst = h_e[:, ic, th * 512:(th + 1) * 512]
                        sl4 = ic % 4
                        if with_p:
                            evb1 = P.add("dve", TT(hdst, bank[:, :],
                                                   p_bc[:, th * 512:(th + 1) * 512], ALU.mult),
                                         deps=[last, ev_pbc_use[0], ev_he_w[ic * 2 + th],
                                               ev_he_read[0]])
                            ev_pbc_read[0] = evb1
                        else:
                            evb1 = P.add("dve", CP(hdst, bank[:, :]),
                                         deps=[last, ev_he_w[ic * 2 + th], ev_he_read[0]])
                        evb2 = P.add("dve", TT(hdst,
                                               a_stage[:, sl4, th * 512:(th + 1) * 512],
                                               hdst, ALU.mult),
                                     deps=[evb1, ev_ast_w[sl4 * 2 + th]])
                        ev_ast_read[sl4 * 2 + th] = evb2
                        ev_he_w[ic * 2 + th] = evb2
                        psg_free[gb] = evb1
                        evs.append(evb2)
            return evs, last

        def down_pass(first, wdram, he_th):
            last_read = None
            for db in range(4):
                sl = db % 2
                ev_wdn_ld[sl] = P.dma("pool", DMA(wdn_sb[sl],
                                                  wdram[:, db * 512:(db + 1) * 512]
                                                  .rearrange("(a p) c -> p a c", p=128)),
                                      deps=[ev_wdn_use[sl], ev_xn2])
                for tt in range(NT):
                    bank = psG[tt % 4]
                    last = None
                    for ic in range(8):
                        dep0 = ([ev_wdn_ld[sl], psg_free[tt % 4]] + he_th[tt // 4]) if ic == 0 else []
                        last = P.add("pe", MM(bank[:, :], h_e[:, ic, tt * 128:(tt + 1) * 128],
                                              wdn_sb[sl][:, ic, :], ic == 0, ic == 7),
                                     deps=dep0)
                    last_read = last
                    dst = comb[:, tt, db * 512:(db + 1) * 512]
                    key = (tt, db)
                    if first:
                        ev_c = P.add("dve", CP(dst, bank[:, :]), deps=[last])
                    else:
                        ev_c = P.add("dve", TT(dst, bank[:, :], dst, ALU.add),
                                     deps=[last, ev_comb_w.get(key)])
                    ev_comb_w[key] = ev_c
                    psg_free[tt % 4] = ev_c
                ev_wdn_use[sl] = last_read
            ev_he_read[0] = last_read

        sh_th = [[], []]
        sh_plan = [(w_sg, 0, True, 0), (w_sg, 1, True, 1), (w_su, 0, False, 0),
                   (w_su, 1, False, 1), (w_sg, 2, True, 2), (w_sg, 3, True, 3),
                   (w_su, 2, False, 2), (w_su, 3, False, 3)]
        for i, (wsrc, blk, is_g, icp) in enumerate(sh_plan):
            sl = i % 2
            wld = load_gu(wsrc[:, blk * 256:(blk + 1) * 256], sl)
            evs, mml = gate_up_block(wld, wgu_sb[sl], icp, is_g, False)
            ev_wgu_use[sl] = mml
            sh_th[0] += evs[0::2]
            sh_th[1] += evs[1::2]
        down_pass(True, w_sd, sh_th)

        for e in range(E):
            ev_prow = None
            for tt in range(NT):
                bank = psG[tt % 4]
                ev_p1 = P.add("pe", TR(bank[0:1, 0:128], probs_tm[:, tt, e:e + 1], id_f32),
                              deps=[ev_probs[tt], psg_free[tt % 4], ev_idf])
                ev_p2 = P.add("dve", CP(p_rows[0:1, tt * 128:(tt + 1) * 128], bank[0:1, 0:128]),
                              deps=[ev_p1, ev_prow_use[0]])
                psg_free[tt % 4] = ev_p2
                ev_prow = ev_p2
            ev_prow_use[0] = ev_prow
            evpb = None
            for th in range(2):
                bank = psG[th]
                ev_q1b = P.add("pe", MM(bank[:, :], ones_f32, p_rows[0:1, th * 512:(th + 1) * 512],
                                        True, True), deps=[ev_prow, psg_free[th], ev_on2])
                evpb = P.add("act", ACP(p_bc[:, th * 512:(th + 1) * 512], bank[:, :]),
                             deps=[ev_q1b, ev_pbc_use[0], ev_pbc_read[0]])
                psg_free[th] = evpb
            ev_pbc_use[0] = evpb
            blocks = [(0, True, 0), (1, True, 1), (4, False, 0), (5, False, 1),
                      (2, True, 2), (3, True, 3), (6, False, 2), (7, False, 3)]
            he_th = [[], []]
            for (blk, is_gate, icpair) in blocks:
                sl = blk % 2
                if blk < 4:
                    colbase = blk * 256
                else:
                    colbase = FI + (blk - 4) * 256
                wld = load_gu(w_gu[e, :, colbase:colbase + 256], sl)
                evs, mml = gate_up_block(wld, wgu_sb[sl], icpair, is_gate, True)
                ev_wgu_use[sl] = mml
                he_th[0] += evs[0::2]
                he_th[1] += evs[1::2]
            down_pass(False, w_dn[e], he_th)

        if "comb" in DBG:
            P.dma("sp", DMA(dbg["comb"], comb.rearrange("p a b -> p (a b)")),
                  deps=[ev_comb_w[(tt, db)] for tt in range(NT) for db in range(4)])
        ev_g = []
        for tt in range(NT):
            deps = [ev_comb_w[(tt, db)] for db in range(4)]
            ev_g.append(P.add("act", ACTF(g_sb[:, tt, :], comb[:, tt, :], AF.Silu), deps=deps))

        if PHASE_LIMIT <= 4:
            P.add("pool", lambda e: None, deps=[ev_g[-1], ev_cc_at_box[0]])
            _emit(P, block, sems)
            return nc

        ev_gi = P.dma("pool", DMA(idxs_sb, gidx.rearrange("a p -> p a", p=128)))
        ev_gath = P.dma("pool", (lambda e: e.indirect_dma_start(
            out=gbuf, out_offset=None,
            in_=ag_at_out.rearrange("(b a) c -> b (a c)", a=64),
            in_offset=bass.IndirectOffsetOnAxis(ap=idxs_sb[:, 0:1], axis=0))),
            deps=[ev_cc_at_box[0], ev_gi] + ev_g)
        ev_cmp = P.dma("pool", DMA(at_compact.rearrange("(p a) c -> p a c", a=64),
                                   gbuf.rearrange("p (a c) -> p a c", c=256)),
                       deps=[ev_gath])
        ev_atfm_l = []
        for r in range(8):
            for s in range(2):
                chunk = r * 2 + s
                ev_atfm_l.append(P.dma("sp", (lambda o=atfm[:, chunk, :],
                                       i=at_compact[r * T:(r + 1) * T, s * 128:(s + 1) * 128]:
                                       (lambda e: e.dma_start_transpose(out=o, in_=i)))(),
                                deps=[ev_cmp] + ev_g))
        ev_atfm = ev_atfm_l[-8:]
        ev_bo = P.dma("pool", DMA(bout_sb, bout_d), deps=ev_g)

        ev_wo_ld = [None, None]
        ev_wo_use = [ev_g[-1], ev_g[-1]]
        ev_xsl = [None, None]
        ev_xsl_use = [None, None]
        ev_out_use = [None, None]
        ev_y = []
        for oc in range(4):
            wsl = oc % 2
            ev_wo_ld[wsl] = P.dma("pool", DMA(wout_sb[wsl],
                                              wout_t[:, oc * 512:(oc + 1) * 512]
                                              .rearrange("(a p) c -> p a c", p=128)),
                                  deps=[ev_wo_use[wsl], ev_cmp])
            lastu = None
            for tt in range(NT):
                k = oc * NT + tt
                xsl = k % 2
                ev_xsl[xsl] = P.dma("sp", DMA(xsl_t[xsl],
                                              x_my[tt * 128:(tt + 1) * 128,
                                                   oc * 512:(oc + 1) * 512]),
                                    deps=[ev_xsl_use[xsl]] + (ev_g if k < 2 else []))
                bank = psG[tt % 4]
                last = None
                for ch in range(16):
                    dep0 = ([ev_wo_ld[wsl], psg_free[tt % 4]] + ev_atfm) if ch == 0 else []
                    last = P.add("pe", MM(bank[:, :], atfm[:, ch, tt * 128:(tt + 1) * 128],
                                          wout_sb[wsl][:, ch, :], ch == 0, False), deps=dep0)
                last = P.add("pe", MM(bank[:, :], ones_bf[0:1, 0:128],
                                      bout_sb[0:1, oc * 512:(oc + 1) * 512], False, True),
                             deps=[ev_bo])
                lastu = last
                ev_o1 = P.add("dve", TT(tmp_f, bank[:, :], g_sb[:, tt, oc * 512:(oc + 1) * 512],
                                        ALU.mult), deps=[last, ev_g[tt], ev_tmpf[0]])
                ev_o2 = P.add("dve", TT(out_t[xsl], tmp_f, xsl_t[xsl], ALU.add),
                              deps=[ev_o1, ev_xsl[xsl], ev_out_use[xsl]])
                ev_tmpf[0] = ev_o2
                psg_free[tt % 4] = ev_o2
                ev_yy = P.dma("sp", DMA(y[tt * 128:(tt + 1) * 128, oc * 512:(oc + 1) * 512],
                                        out_t[xsl]), deps=[ev_o2])
                ev_out_use[xsl] = ev_yy
                ev_xsl_use[xsl] = ev_o2
                ev_y.append(ev_yy)
            ev_wo_use[wsl] = lastu

        P.add("pool", lambda e: None, deps=P.dma_all("sp"))

        _emit(P, block, sems)

    return nc


def _emit(P, block, sems):
    def mk(eng):
        def body(obj):
            P.emit_engine(eng, obj, sems)
        return body

    block.tensor(mk("pe"))
    block.scalar(mk("act"))
    block.vector(mk("dve"))
    block.gpsimd(mk("pool"))
    block.sync(mk("sp"))


_NC_CACHE = {}
_LAST_RES = None


def _prep_inputs(inputs):
    g = {k: np.asarray(v, dtype=np.float32) for k, v in inputs.items()}
    x = g["L0_attn_hidden_states"].reshape(S, D)
    qkv_w = g["L0_attn_qkv_w"]
    qkv_b = g["L0_attn_qkv_b"]
    out_w = g["L0_attn_out_w"]
    out_b = g["L0_attn_out_b"]
    slopes = g["L0_attn_alibi_slopes"]

    perm = []
    for r in range(8):
        for sl in range(2):
            h = 8 * sl + r
            perm.extend(range(h * 128, (h + 1) * 128))
    wout_t_perm = np.ascontiguousarray(out_w.T[perm, :])

    shared = {
        "w_router": np.ascontiguousarray(g["L0_ffn_router_w"]),
        "w_gu": np.ascontiguousarray(g["L0_ffn_gate_up_w"]),
        "w_dn": np.ascontiguousarray(g["L0_ffn_down_w"]),
        "w_sg": np.ascontiguousarray(g["L0_ffn_shared_gate_w"]),
        "w_su": np.ascontiguousarray(g["L0_ffn_shared_up_w"]),
        "w_sd": np.ascontiguousarray(g["L0_ffn_shared_down_w"]),
        "wout_t": wout_t_perm,
        "bout": np.ascontiguousarray(out_b.reshape(1, D)),
    }
    maps = []
    for c in range(NCORE):
        rows = np.r_[c * 128:(c + 1) * 128, (8 + c) * 128:(9 + c) * 128]
        m = dict(shared)
        m["x_my"] = np.ascontiguousarray(x[c * T:(c + 1) * T, :])
        m["wq_t"] = np.ascontiguousarray(qkv_w[rows, :].T)
        m["wk_t"] = np.ascontiguousarray(qkv_w[2048 + rows, :].T)
        m["wv_t"] = np.ascontiguousarray(qkv_w[4096 + rows, :].T)
        m["bq"] = np.ascontiguousarray(qkv_b[rows].reshape(1, 256))
        m["bk"] = np.ascontiguousarray(qkv_b[2048 + rows].reshape(1, 256))
        m["bv"] = np.ascontiguousarray(qkv_b[4096 + rows].reshape(1, 256))
        m["slopes2"] = np.ascontiguousarray(slopes[[c, 8 + c]].reshape(1, 2))
        gi = np.array([r * 128 + c * 16 + j for r in range(8) for j in range(16)],
                      dtype=np.uint32).reshape(1, 128)
        m["gidx"] = gi
        maps.append(m)
    return maps


def kernel(**inputs):
    key = ("main", tuple(sorted(DBG)), PHASE_LIMIT)
    if key not in _NC_CACHE:
        _NC_CACHE[key] = build_nc()
    nc = _NC_CACHE[key]
    maps = _prep_inputs(inputs)
    res = run_bass_kernel_spmd(nc, maps, core_ids=list(range(NCORE)))
    global _LAST_RES
    _LAST_RES = res
    out = np.concatenate([np.asarray(res.results[c]["y"]) for c in range(NCORE)], axis=0)
    return out.reshape(1, S, D).astype(np.float32)


if __name__ == "__main__":
    nc = build_nc()
    ni = sum(1 for _ in nc.all_instructions())
    print("built ok, instructions:", ni)
